# version 02 (final)
# baseline (speedup 1.0000x reference)
import numpy as np
import ml_dtypes
from contextlib import ExitStack

import concourse.bass as bass
import concourse.mybir as mybir
from concourse.bass_utils import run_bass_kernel_spmd

F32 = mybir.dt.float32
BF16 = mybir.dt.bfloat16
AF = mybir.ActivationFunctionType
ALU = mybir.AluOpType
AX = mybir.AxisListType
BF = ml_dtypes.bfloat16

NCORES = 8
D = 2048
TPC = 1024
B, S = 2, 4096
SCALE0 = 1.0 / float(np.sqrt(128.0))
LN_EPS = 1e-5
NDC = D // 128


class Em:
    ENGINES = ("sync", "scalar", "vector", "tensor", "gpsimd")

    def __init__(self, nc, ctx):
        self.nc = nc
        self.ctx = ctx
        self.streams = {e: [] for e in self.ENGINES}
        self.sem = {e: ctx.enter_context(nc.semaphore(f"sem_{e}")) for e in self.ENGINES}
        self.cnt = {e: 0 for e in self.ENGINES}
        self.last_chain = {}
        self.waited = {e: {f: 0 for f in self.ENGINES} for e in self.ENGINES}

    CHAIN = {"vector", "scalar", "gpsimd"}

    def emit(self, eng, fn, deps=(), mark=False, dma=False):
        chain = (eng in self.CHAIN) and not dma
        if chain:
            mark = True
            lc = self.last_chain.get(eng)
            if lc:
                deps = list(deps) + [(eng, lc)]
        for d in deps:
            if d is None:
                continue
            se, val = d
            if self.waited[eng][se] < val:
                self.waited[eng][se] = val
                semh = self.sem[se]
                self.streams[eng].append(lambda E, s=semh, v=val: E.wait_ge(s, v))
        if mark:
            inc = 16 if dma else 1
            self.cnt[eng] += inc
            tokval = self.cnt[eng]
            semh = self.sem[eng]
            self.streams[eng].append(lambda E, f=fn, s=semh, i=inc: f(E).then_inc(s, i))
            if chain:
                self.last_chain[eng] = tokval
            return (eng, tokval)
        self.streams[eng].append(fn)
        return None

    def sb(self, name, shape, dtype):
        return self.ctx.enter_context(self.nc.sbuf_tensor(name, shape, dtype))

    def ps(self, name, shape, dtype=F32):
        return self.ctx.enter_context(self.nc.psum_tensor(name, shape, dtype))

    def finalize(self):
        for e in self.ENGINES:
            if self.cnt[e] > 0:
                semh, v = self.sem[e], self.cnt[e]
                self.streams[e].append(lambda E, s=semh, vv=v: E.wait_ge(s, vv))

    def replay(self):
        self.finalize()
        with self.nc.Block() as block:
            for name, starter in (("sync", block.sync), ("scalar", block.scalar),
                                  ("vector", block.vector), ("tensor", block.tensor),
                                  ("gpsimd", block.gpsimd)):
                stream = self.streams[name]
                if stream:
                    starter(lambda E, st=stream: [f(E) for f in st])


def build_program(builder, **kw):
    nc = bass.Bass("TRN2", target_bir_lowering=False)
    with ExitStack() as ctx:
        em = Em(nc, ctx)
        builder(em, **kw)
        em.replay()
    return nc


def load_consts(em, need_ident=True, need_ones=False):
    nc = em.nc
    c = {}
    if need_ident:
        ident = nc.inline_tensor(np.eye(128, dtype=np.float32).astype(BF), "c_ident")
        c["ident"] = em.sb("c_ident_sb", [128, 128], BF16)
        c["ident_tok"] = em.emit("sync", lambda E, d=ident, s=c["ident"]:
                                 E.dma_start(out=s[:], in_=d.ap()), mark=True, dma=True)
    if need_ones:
        ones_col = nc.inline_tensor(np.ones((128, 1), dtype=np.float32).astype(BF), "c_onescol")
        ones_row = nc.inline_tensor(np.ones((1, 128), dtype=np.float32).astype(BF), "c_onesrow")
        c["ones_col"] = em.sb("c_onescol_sb", [128, 1], BF16)
        c["ones_row"] = em.sb("c_onesrow_sb", [1, 128], BF16)
        em.emit("sync", lambda E, d=ones_col, s=c["ones_col"]:
                E.dma_start(out=s[:], in_=d.ap()), mark=True, dma=True)
        c["ones_tok"] = em.emit("sync", lambda E, d=ones_row, s=c["ones_row"]:
                                E.dma_start(out=s[:], in_=d.ap()), mark=True, dma=True)
    return c


def emit_ln_tile(em, xt_ap, xt_tok, nfeat, bufs, out_ap, prev_apply=None,
                 out_war=None, eps=LN_EPS):
    cols = bufs["cols"]
    xsq = bufs["xsq"]
    cS, cQ, cM, cV, cR, cB = (cols[:, i:i + 1] for i in range(6))
    v, s = "vector", "scalar"
    t0 = em.emit(v, lambda E: E.reduce_sum(cS, xt_ap, axis=AX.X),
                 deps=[xt_tok, prev_apply], mark=True)
    t1 = em.emit(s, lambda E: E.activation(xsq[:], xt_ap, AF.Square,
                                           accum_out=cQ), deps=[xt_tok, prev_apply], mark=True)
    a = em.emit(v, lambda E: E.tensor_scalar_mul(cM, cS, 1.0 / nfeat),
                deps=[t0], mark=True)
    b = em.emit(v, lambda E: E.tensor_scalar_mul(cV, cQ, 1.0 / nfeat),
                deps=[t1], mark=True)
    c = em.emit(v, lambda E: E.tensor_mul(cR, cM, cM), deps=[a], mark=True)
    d = em.emit(v, lambda E: E.tensor_sub(cV, cV, cR), deps=[b, c], mark=True)
    t3 = em.emit(v, lambda E: E.tensor_scalar_add(cV, cV, eps), deps=[d], mark=True)
    t4 = em.emit(s, lambda E: E.activation(cR, cV, AF.Sqrt), deps=[t3], mark=True)
    t5 = em.emit(v, lambda E: E.reciprocal(cR, cR), deps=[t4], mark=True)
    t7 = em.emit(v, lambda E: E.tensor_scalar(out_ap, xt_ap, cM, cR,
                                              op0=ALU.subtract, op1=ALU.mult),
                 deps=[t5] + (out_war or []), mark=True)
    return t0, t7


def emit_transpose128(em, src_bf, consts, psT, slot, dst_bf, deps):
    pslice = psT[slot][:, 0:128]
    tt = em.emit("tensor", lambda E, p=pslice: E.transpose(p, src_bf, consts["ident"][:]),
                 deps=deps, mark=True)
    ev = em.emit("vector", lambda E, p=pslice: E.tensor_copy(dst_bf, p), deps=[tt], mark=True)
    return tt, ev


def emit_lnT(em, consts, x_src, nrows, xlnT, bufs, psT, x_resident=None,
             ln_out_tm=None, first_deps=()):
    ntiles = nrows // 128
    xln = bufs["xln"]
    nslot = len(psT)
    xt_rd = [[], []]
    slot_ev = [None] * nslot
    slot_tr = [None] * nslot
    prev_apply = None
    last_tr_of_prev = None
    for i in range(ntiles):
        bi = i % 2
        if x_resident is not None:
            xt_view = x_resident[:, i * D:(i + 1) * D]
            ld = None
        else:
            xt_view = bufs["xtile"][bi][:]
            ld = em.emit("sync", lambda E, x=xt_view, i0=i: E.dma_start(
                out=x, in_=x_src[i0 * 128:(i0 + 1) * 128, :]),
                deps=list(xt_rd[bi]) + list(first_deps if i < 2 else ()),
                mark=True, dma=True)
        if ln_out_tm is None:
            out_base, out_off = xln, 0
        else:
            out_base, out_off = ln_out_tm, i * D
        out_view = out_base[:, out_off:out_off + D]
        t_stats, t_apply = emit_ln_tile(
            em, xt_view, ld, D, bufs, out_view, prev_apply=prev_apply,
            out_war=([last_tr_of_prev] if ln_out_tm is None else []))
        prev_apply = t_apply
        for d in range(NDC):
            slot = d % nslot
            tr, ev = emit_transpose128(
                em, out_base[:, out_off + d * 128:out_off + (d + 1) * 128],
                consts, psT, slot,
                xlnT[:, d * nrows + i * 128: d * nrows + (i + 1) * 128],
                deps=[t_apply, slot_ev[slot], consts.get("ident_tok")])
            slot_tr[slot] = tr
            slot_ev[slot] = ev
        last_tr_of_prev = slot_tr[(NDC - 1) % nslot]
        xt_rd[bi] = [t_stats, t_apply]
    return [t for t in slot_ev if t is not None] + [prev_apply]


def build_l0a(em):
    nc = em.nc
    xs = nc.dram_tensor("xs", [TPC, D], F32, kind="ExternalInput").ap()
    qw = nc.dram_tensor("qw", [D, D], F32, kind="ExternalInput").ap()
    kw = nc.dram_tensor("kw", [D, D], F32, kind="ExternalInput").ap()
    vw = nc.dram_tensor("vw", [D, D], F32, kind="ExternalInput").ap()
    Qf = nc.dram_tensor("Qf", [D, TPC], BF16, kind="ExternalOutput").ap()
    Kf = nc.dram_tensor("Kf", [D, TPC], BF16, kind="ExternalOutput").ap()
    Vt = nc.dram_tensor("Vt", [TPC, D], BF16, kind="ExternalOutput").ap()

    consts = load_consts(em)

    xlnT = em.sb("xlnT", [128, NDC * TPC], BF16)
    bufs = {
        "xtile": [em.sb(f"xtile{i}", [128, D], F32) for i in range(2)],
        "xln": em.sb("xln", [128, D], BF16),
        "xsq": em.sb("xsq", [128, D], F32),
        "cols": em.sb("cols", [128, 8], F32),
    }
    wbuf = [em.sb(f"wbuf{i}", [128, NDC * 128], BF16) for i in range(2)]
    vwbuf = [em.sb(f"vwbuf{i}", [128, NDC * 512], BF16) for i in range(2)]
    ostage = [em.sb(f"ostage{i}", [128, 512], BF16) for i in range(2)]

    psT = [em.ps(f"psT{i}", [128, 1024], BF16) for i in range(2)]
    psmm = [em.ps(f"psmm{i}", [128, 512], F32) for i in range(2)]

    xln_done = emit_lnT(em, consts, xs, TPC, xlnT, bufs, psT)

    war = {"wb": [None, None], "stage": [None, None], "ps": [None, None],
           "vb": [None, None]}

    def fm_proj(wdram, outdram, scale):
        wb_last = war["wb"]
        stage_last = war["stage"]
        ps_last = war["ps"]
        for e in range(NDC):
            wb = wbuf[e % 2]
            wl = em.emit("gpsimd", lambda E, w=wb, e0=e, src=wdram: E.dma_start(
                out=w[:].rearrange("p (c e) -> p c e", c=NDC),
                in_=src[:, e0 * 128:(e0 + 1) * 128]
                .rearrange("(c p) e -> p c e", p=128)),
                deps=[wb_last[e % 2]] + xln_done, mark=True, dma=True)
            for tt in range(2):
                psu = psmm[tt]
                mm = None
                for d in range(NDC):
                    mm = em.emit("tensor", lambda E, p=psu, w=wb, d0=d, t0=tt: E.matmul(
                        p[:, 0:512], w[:, d0 * 128:(d0 + 1) * 128],
                        xlnT[:, d0 * TPC + t0 * 512: d0 * TPC + t0 * 512 + 512],
                        start=(d0 == 0), stop=(d0 == NDC - 1)),
                        deps=([wl, ps_last[tt]] if d == 0 else []),
                        mark=(d == NDC - 1))
                ostg = ostage[tt]
                ev = em.emit("scalar", lambda E, o=ostg, p=psu: E.activation(
                    o[:], p[:, 0:512], AF.Copy, scale=scale),
                    deps=[mm, stage_last[tt]], mark=True)
                ps_last[tt] = ev
                st = em.emit("sync", lambda E, o=ostg, e0=e, t0=tt, dst=outdram:
                             E.dma_start(out=dst[e0 * 128:(e0 + 1) * 128,
                                                 t0 * 512:(t0 + 1) * 512], in_=o[:]),
                             deps=[ev], mark=True, dma=True)
                stage_last[tt] = st
            wb_last[e % 2] = mm
        return

    fm_proj(qw, Qf, SCALE0)
    fm_proj(kw, Kf, 1.0)

    vb_last = war["vb"]
    stage_last = war["stage"]
    ps_last = war["ps"]
    for p in range(4):
        vb = vwbuf[p % 2]
        wl = em.emit("gpsimd", lambda E, w=vb, p0=p: E.dma_start(
            out=w[:].rearrange("p (c e) -> p c e", c=NDC),
            in_=vw[:, p0 * 512:(p0 + 1) * 512]
            .rearrange("(c p) e -> p c e", p=128)),
            deps=[vb_last[p % 2]] + xln_done, mark=True, dma=True)
        for tc in range(8):
            pi = tc % 2
            psu = psmm[pi]
            mm = None
            for d in range(NDC):
                mm = em.emit("tensor", lambda E, ps_=psu, w=vb, d0=d, t0=tc: E.matmul(
                    ps_[:, 0:512],
                    xlnT[:, d0 * TPC + t0 * 128: d0 * TPC + (t0 + 1) * 128],
                    w[:, d0 * 512:(d0 + 1) * 512],
                    start=(d0 == 0), stop=(d0 == NDC - 1)),
                    deps=([wl, ps_last[pi]] if d == 0 else []),
                    mark=(d == NDC - 1))
            ostg = ostage[pi]
            ev = em.emit("vector", lambda E, o=ostg, ps_=psu: E.tensor_copy(o[:], ps_[:, 0:512]),
                         deps=[mm, stage_last[pi]], mark=True)
            ps_last[pi] = ev
            st = em.emit("sync", lambda E, o=ostg, p0=p, t0=tc: E.dma_start(
                out=Vt[t0 * 128:(t0 + 1) * 128, p0 * 512:(p0 + 1) * 512], in_=o[:]),
                deps=[ev], mark=True, dma=True)
            stage_last[pi] = st
        vb_last[p % 2] = mm


NPAIR = 4
SQ = 4096


def build_l0b(em, NPAIR=NPAIR, SQ=SQ, debug=False):
    nc = em.nc
    Qh = nc.dram_tensor("Qh", [NPAIR, 128, SQ], BF16, kind="ExternalInput").ap()
    Kh = nc.dram_tensor("Kh", [NPAIR, 128, SQ], BF16, kind="ExternalInput").ap()
    Vh = nc.dram_tensor("Vh", [NPAIR, SQ, 128], BF16, kind="ExternalInput").ap()
    gwT = nc.dram_tensor("gwT", [128, 128], F32, kind="ExternalInput").ap()
    gb = nc.dram_tensor("gb", [128, 1], F32, kind="ExternalInput").ap()
    mstrip = nc.dram_tensor("mstrip", [128, 896], BF16, kind="ExternalInput").ap()
    Oh = nc.dram_tensor("Oh", [NPAIR, 128, SQ], BF16, kind="ExternalOutput").ap()
    if debug:
        dbg_negM = nc.dram_tensor("dbg_negM", [1, SQ], BF16, kind="ExternalOutput").ap()
        dbg_srow = nc.dram_tensor("dbg_srow", [SQ // 512, 512], F32, kind="ExternalOutput").ap()
        dbg_P = nc.dram_tensor("dbg_P", [128, 512], BF16, kind="ExternalOutput").ap()
        dbg_av = nc.dram_tensor("dbg_av", [128, 512], F32, kind="ExternalOutput").ap()
        dbg_gate = nc.dram_tensor("dbg_gate", [128, 512], F32, kind="ExternalOutput").ap()

    consts = load_consts(em, need_ident=False, need_ones=True)
    ones_c, ones_r = consts["ones_col"], consts["ones_row"]

    Qsb = [em.sb(f"Qsb{i}", [128, SQ], BF16) for i in range(2)]
    Ksb = [em.sb(f"Ksb{i}", [128, SQ], BF16) for i in range(2)]
    Vsb = [em.sb(f"Vsb{i}", [128, SQ], BF16) for i in range(2)]
    gwsb = em.sb("gwsb", [128, 128], BF16)
    gbsb = em.sb("gbsb", [128, 1], F32)
    strip = em.sb("strip", [128, 896], BF16)
    sqbuf = em.sb("sqbuf", [128, SQ], BF16)
    qnrow = em.sb("qnrow", [1, SQ], F32)
    knrow = em.sb("knrow", [1, SQ], F32)
    km = em.sb("km", [1, 2], F32)
    negM = em.sb("negM", [1, SQ], BF16)
    Pb = [em.sb(f"Pb{i}", [128, 512], BF16) for i in range(2)]
    srow = em.sb("srow", [1, 512], F32)
    srow_bf = em.sb("srow_bf", [1, 512], BF16)
    gate_sb = em.sb("gate_sb", [128, 512], F32)
    tmp1 = em.sb("tmp1", [128, 512], F32)
    ostage = [em.sb(f"ostage{i}", [128, 512], BF16) for i in range(2)]

    psS2 = [em.ps(f"psS2{i}", [128, 512], F32) for i in range(2)]
    psAV = em.ps("psAV", [128, 512], F32)
    psSum = em.ps("psSum", [128, 512], F32)
    psBC = em.ps("psBC", [128, 512], F32)
    psG = em.ps("psG", [128, 512], F32)

    t_gw = em.emit("gpsimd", lambda E: E.dma_start(out=gwsb[:], in_=gwT),
                   mark=True, dma=True)
    t_gb = em.emit("sync", lambda E: E.dma_start(out=gbsb[:], in_=gb), mark=True, dma=True)
    t_strip = em.emit("sync", lambda E: E.dma_start(out=strip[:], in_=mstrip),
                      mark=True, dma=True)

    war = {"S": [None, None], "P": [None, None], "AV": None, "Sum": None,
           "BC": None, "G": None, "stage": [None, None], "QKV": [[], []]}

    for i in range(NPAIR):
        bi = i % 2
        qs, ks, vs = Qsb[bi], Ksb[bi], Vsb[bi]
        ldq = em.emit("sync", lambda E, d=qs, i0=i: E.dma_start(out=d[:], in_=Qh[i0]),
                      deps=war["QKV"][bi], mark=True, dma=True)
        ldk = em.emit("sync", lambda E, d=ks, i0=i: E.dma_start(out=d[:], in_=Kh[i0]),
                      deps=war["QKV"][bi], mark=True, dma=True)
        ldv = em.emit("sync", lambda E, d=vs, i0=i: E.dma_start(
            out=d[:].rearrange("p (c e) -> p c e", c=SQ // 128),
            in_=Vh[i0].rearrange("(c p) e -> p c e", p=128)),
            deps=war["QKV"][bi], mark=True, dma=True)

        tk = em.emit("scalar", lambda E, s=ks: E.activation(sqbuf[:], s[:], AF.Square),
                     deps=[ldk], mark=True)
        rowtoks = []
        for j in range(SQ // 512):
            mmn = em.emit("tensor", lambda E, j0=j: E.matmul(
                psSum[0:1, 0:512], ones_c[:], sqbuf[:, j0 * 512:(j0 + 1) * 512],
                start=True, stop=True),
                deps=[tk, war["Sum"]] + rowtoks[-1:], mark=True)
            cp = em.emit("vector", lambda E, j0=j: E.tensor_copy(
                knrow[0:1, j0 * 512:(j0 + 1) * 512], psSum[0:1, 0:512]),
                deps=[mmn], mark=True)
            rowtoks.append(cp)
        tq = em.emit("scalar", lambda E, s=qs: E.activation(sqbuf[:], s[:], AF.Square),
                     deps=[ldq, rowtoks[-1]] + [t for t in rowtoks], mark=True)
        for j in range(SQ // 512):
            mmn = em.emit("tensor", lambda E, j0=j: E.matmul(
                psSum[0:1, 0:512], ones_c[:], sqbuf[:, j0 * 512:(j0 + 1) * 512],
                start=True, stop=True),
                deps=[tq] + rowtoks[-1:], mark=True)
            cp = em.emit("vector", lambda E, j0=j: E.tensor_copy(
                qnrow[0:1, j0 * 512:(j0 + 1) * 512], psSum[0:1, 0:512]),
                deps=[mmn], mark=True)
            rowtoks.append(cp)
        tmax = em.emit("vector", lambda E: E.reduce_max(km[0:1, 0:1], knrow[0:1, :], axis=AX.X))
        tk2 = em.emit("vector", lambda E: E.tensor_scalar_mul(km[0:1, 1:2], km[0:1, 0:1],
                                                              -0.5 * SCALE0))
        tq2 = em.emit("vector", lambda E: E.tensor_scalar_mul(qnrow[0:1, :], qnrow[0:1, :],
                                                              -0.5 / SCALE0))
        tnm = em.emit("vector", lambda E: E.tensor_scalar(negM[0:1, :], qnrow[0:1, :],
                                                          km[0:1, 1:2], None, op0=ALU.add),
                      deps=[war.get("negM")], mark=True)
        war["Sum"] = tnm

        for j in range(SQ // 512):
            nkc = (j + 1) * 4
            exp_toks = []
            avmm = None
            summm = None
            for ci in range(nkc):
                sb_i = ci % 2
                m1 = em.emit("tensor", lambda E, c0=ci, j0=j, pp=psS2[sb_i], ks_=ks, qs_=qs:
                             E.matmul(pp[:, 0:512], ks_[:, c0 * 128:(c0 + 1) * 128],
                                      qs_[:, j0 * 512:(j0 + 1) * 512],
                                      start=True, stop=False),
                             deps=[tnm, ldq, war["S"][sb_i]], mark=False)
                m2 = em.emit("tensor", lambda E, j0=j, pp=psS2[sb_i]:
                             E.matmul(pp[:, 0:512], ones_r[:],
                                      negM[0:1, j0 * 512:(j0 + 1) * 512],
                                      start=False, stop=True), mark=True)
                ex = em.emit("scalar", lambda E, pp=psS2[sb_i], pb=Pb[sb_i]:
                             E.activation(pb[:], pp[:, 0:512], AF.Exp),
                             deps=[m2, war["P"][sb_i]], mark=True)
                war["S"][sb_i] = ex
                ptok = ex
                r = ci * 128 - j * 512
                if r >= 0:
                    ptok = em.emit("vector", lambda E, pb=Pb[sb_i], r0=r:
                                   E.tensor_mul(pb[:], pb[:],
                                                strip[:, 384 - r0:384 - r0 + 512]),
                                   deps=[ex, t_strip], mark=True)
                avmm = em.emit("tensor", lambda E, c0=ci, pb=Pb[sb_i], vs_=vs, st_=(ci == 0), sp_=(ci == nkc - 1):
                               E.matmul(psAV[:, 0:512], vs_[:, c0 * 128:(c0 + 1) * 128],
                                        pb[:], start=st_, stop=sp_),
                               deps=[ptok, ldv] + ([war["AV"]] if ci == 0 else []),
                               mark=True)
                summm = em.emit("tensor", lambda E, pb=Pb[sb_i], st_=(ci == 0), sp_=(ci == nkc - 1):
                                E.matmul(psSum[0:1, 0:512], ones_c[:], pb[:],
                                         start=st_, stop=sp_),
                                deps=([war["Sum"]] if ci == 0 else []),
                                mark=True)
                war["P"][sb_i] = summm
                last_m2 = m2
                if debug and i == 0 and j == 0 and ci == 0:
                    em.emit("sync", lambda E, pb=Pb[sb_i]: E.dma_start(
                        out=dbg_P, in_=pb[:]), deps=[ptok], mark=True, dma=True)
            cp = em.emit("vector", lambda E: E.tensor_copy(srow[0:1, :], psSum[0:1, 0:512]),
                         deps=[summm], mark=True)
            war["Sum"] = cp
            if debug and i == 0:
                em.emit("sync", lambda E, j0=j: E.dma_start(
                    out=dbg_srow[j0:j0+1, :], in_=srow[0:1, :]), deps=[cp], mark=True, dma=True)
            em.emit("vector", lambda E: E.reciprocal(srow[0:1, :], srow[0:1, :]))
            rc = em.emit("vector", lambda E: E.tensor_copy(srow_bf[0:1, :], srow[0:1, :]),
                         mark=True)
            bc = em.emit("tensor", lambda E: E.matmul(psBC[:, 0:512], ones_r[:],
                                                      srow_bf[0:1, :], start=True, stop=True),
                         deps=[rc, war["BC"]], mark=True)
            gp = em.emit("tensor", lambda E, j0=j, qs_=qs: E.matmul(
                psG[:, 0:512], gwsb[:], qs_[:, j0 * 512:(j0 + 1) * 512],
                start=True, stop=True), deps=[t_gw, war["G"]], mark=True)
            sg = em.emit("scalar", lambda E: E.activation(gate_sb[:], psG[:, 0:512],
                                                          AF.Sigmoid, bias=gbsb[:, 0:1],
                                                          scale=1.0 / SCALE0),
                         deps=[gp, t_gb], mark=True)
            war["G"] = sg
            if debug and i == 0 and j == 0:
                dav = em.emit("vector", lambda E: E.tensor_copy(tmp1[:], psAV[:, 0:512]),
                              deps=[avmm], mark=True)
                em.emit("sync", lambda E: E.dma_start(out=dbg_av, in_=tmp1[:]),
                        deps=[dav], mark=True, dma=True)
                em.emit("sync", lambda E: E.dma_start(out=dbg_gate, in_=gate_sb[:]),
                        deps=[sg], mark=True, dma=True)
            mu1 = em.emit("vector", lambda E: E.tensor_mul(tmp1[:], psAV[:, 0:512], gate_sb[:]),
                          deps=[avmm, sg], mark=True)
            war["AV"] = mu1
            ost = ostage[j % 2]
            mu2 = em.emit("vector", lambda E, o=ost: E.tensor_mul(o[:], tmp1[:], psBC[:, 0:512]),
                          deps=[bc, war["stage"][j % 2]], mark=True)
            war["BC"] = mu2
            st = em.emit("sync", lambda E, o=ost, i0=i, j0=j: E.dma_start(
                out=Oh[i0][:, j0 * 512:(j0 + 1) * 512], in_=o[:]),
                deps=[mu2], mark=True, dma=True)
            war["stage"][j % 2] = st
        if debug and i == 0:
            em.emit("sync", lambda E: E.dma_start(out=dbg_negM, in_=negM[0:1, :]),
                    deps=[tnm], mark=True, dma=True)
        war["negM"] = last_m2
        war["QKV"][bi] = [avmm, summm, war["G"], tq]


def run_l0b(Qf_all, Kf_all, Vt_all, gwT, gb, mstrip):
    nc2 = _get_prog("l0b", build_l0b)
    in_maps = []
    for c in range(NCORES):
        pairs = [((4 * c + j) // NH0, (4 * c + j) % NH0) for j in range(NPAIR)]
        Q = np.stack([Qf_all[b][128 * h:128 * (h + 1), :] for (b, h) in pairs])
        Kk = np.stack([Kf_all[b][128 * h:128 * (h + 1), :] for (b, h) in pairs])
        V = np.stack([np.ascontiguousarray(Vt_all[b][:, 128 * h:128 * (h + 1)])
                      for (b, h) in pairs])
        in_maps.append({"Qh": Q, "Kh": Kk, "Vh": V, "gwT": gwT, "gb": gb,
                        "mstrip": mstrip})
    return run_bass_kernel_spmd(nc2, in_maps, core_ids=list(range(NCORES)))


def make_mask_strip():
    kk = np.arange(128)[:, None]
    cc = np.arange(896)[None, :]
    return (kk <= cc - 384).astype(np.float32).astype(BF)


NH0 = 16
MOE_EPS = 1e-8


def build_l0c(em, debug=False):
    nc = em.nc
    xs = nc.dram_tensor("xs", [TPC, D], F32, kind="ExternalInput").ap()
    gfm = nc.dram_tensor("gfm", [D, TPC], BF16, kind="ExternalInput").ap()
    ow = nc.dram_tensor("ow", [D, D], F32, kind="ExternalInput").ap()
    rw = nc.dram_tensor("rw", [D, 128], F32, kind="ExternalInput").ap()
    amp = nc.dram_tensor("amp", [128, 16], F32, kind="ExternalInput").ap()
    escale = nc.dram_tensor("escale", [128, 1], F32, kind="ExternalInput").ap()
    wg = nc.dram_tensor("wg", [D, 8192], F32, kind="ExternalInput").ap()
    wu = nc.dram_tensor("wu", [D, 8192], F32, kind="ExternalInput").ap()
    wd = nc.dram_tensor("wd", [1024, D], F32, kind="ExternalInput").ap()
    swg = nc.dram_tensor("swg", [D, D], F32, kind="ExternalInput").ap()
    swu = nc.dram_tensor("swu", [D, D], F32, kind="ExternalInput").ap()
    swd = nc.dram_tensor("swd", [D, D], F32, kind="ExternalInput").ap()
    x2o = nc.dram_tensor("x2o", [TPC, D], F32, kind="ExternalOutput").ap()
    x1s = nc.dram_tensor("x1s", [TPC, D], F32,
                         kind="ExternalOutput" if debug else "Internal").ap()
    if debug:
        dbg_probs = nc.dram_tensor("dbg_probs", [8, 128, 128], F32, kind="ExternalOutput").ap()
        dbg_alpha = nc.dram_tensor("dbg_alpha", [128, 64], F32, kind="ExternalOutput").ap()
        dbg_comb = nc.dram_tensor("dbg_comb", [128, 8192], F32, kind="ExternalOutput").ap()
        dbg_mix2 = nc.dram_tensor("dbg_mix2", [128, 8], BF16, kind="ExternalOutput").ap()
        dbg_xlnT = nc.dram_tensor("dbg_xlnT", [128, NDC * TPC], BF16, kind="ExternalOutput").ap()

    consts = load_consts(em, need_ident=True, need_ones=False)

    xlnT = em.sb("xlnT", [128, NDC * TPC], BF16)
    combined = em.sb("combined", [128, 8 * 1024], F32)
    arena = em.sb("arena", [128, 36864], BF16)
    x1t = [em.sb(f"x1t{i}", [128, D], F32) for i in range(2)]
    bufs = {
        "xln": em.sb("xln", [128, D], BF16),
        "xsq": em.sb("xsq", [128, D], F32),
        "cols": em.sb("cols", [128, 8], F32),
    }
    rwb = em.sb("rwb", [128, NDC * 128], BF16)
    probs = em.sb("probs", [128, 128], F32)
    zapped = em.sb("zapped", [128, 128], F32)
    selbf = em.sb("selbf", [128, 128], BF16)
    selT = em.sb("selT", [128, 128], BF16)
    m8 = em.sb("m8", [128, 8], F32)
    rcols = em.sb("rcols", [128, 8], F32)
    alpha = em.sb("alpha", [128, 8 * 8], F32)
    mixb = em.sb("mixb", [128, 16], F32)
    mix2 = em.sb("mix2", [128, 8], BF16)
    sil = em.sb("sil", [128, 512], F32)
    basis = em.sb("basis", [128, 512], F32)
    xst = em.sb("xst", [128, 512], F32)

    psT = [em.ps(f"psT{i}", [128, 1024], BF16) for i in range(2)]
    psO = em.ps("psO", [128, 512], F32)
    psG = em.ps("psG", [128, 512], F32)
    psU = em.ps("psU", [128, 512], F32)
    psR = em.ps("psR", [128, 512], F32)

    OW, GT = 0, 32768
    WG, WU = 0, 8192
    PR = 16384
    SW = 32768
    CT, SWD, WD = 0, 8192, 32768

    t_rw = em.emit("gpsimd", lambda E: E.dma_start(
        out=rwb[:].rearrange("p (c e) -> p c e", c=NDC),
        in_=rw.rearrange("(c p) e -> p c e", p=128)), mark=True, dma=True)
    t_amp = em.emit("sync", lambda E: E.dma_start(out=mixb[:], in_=amp), mark=True, dma=True)
    t_esc = em.emit("sync", lambda E: E.dma_start(out=rcols[:, 7:8], in_=escale),
                    mark=True, dma=True)
    tm0 = em.emit("scalar", lambda E: E.activation(mixb[:], mixb[:], AF.Square),
                  deps=[t_amp], mark=True)
    ev = mixb[:].rearrange("p (k two) -> p k two", two=2)
    tm1 = em.emit("vector", lambda E: E.tensor_add(m8[:], ev[:, :, 0], ev[:, :, 1]),
                  deps=[tm0], mark=True)
    tm2 = em.emit("vector", lambda E: E.reduce_sum(rcols[:, 6:7], m8[:], axis=AX.X), mark=True)
    tm3 = em.emit("vector", lambda E: E.tensor_scalar_add(rcols[:, 6:7], rcols[:, 6:7], MOE_EPS), mark=True)
    tm4 = em.emit("vector", lambda E: E.reciprocal(rcols[:, 6:7], rcols[:, 6:7]), mark=True)
    tm5 = em.emit("vector", lambda E: E.tensor_scalar(m8[:], m8[:], rcols[:, 6:7],
                                                      rcols[:, 7:8], op0=ALU.mult, op1=ALU.mult),
                  deps=[tm4, t_esc], mark=True)
    t_mix = em.emit("vector", lambda E: E.tensor_copy(mix2[:], m8[:]), mark=True)

    t_ow = em.emit("gpsimd", lambda E: E.dma_start(
        out=arena[:, OW:OW + 32768].rearrange("p (c e) -> p c e", c=NDC),
        in_=ow.rearrange("(c p) e -> p c e", p=128)), mark=True, dma=True)

    xt_rd = [[], []]
    gfm_rd = [None, None]
    ln_prev = None
    last_tr = None
    slot_ev = [None, None]
    slot_tr = [None, None]
    psO_rd = None
    for tc in range(8):
        bi = tc % 2
        xt = x1t[bi]
        ldx = em.emit("sync", lambda E, x=xt, t0=tc: E.dma_start(
            out=x[:], in_=xs[t0 * 128:(t0 + 1) * 128, :]), deps=xt_rd[bi], mark=True, dma=True)
        greg = GT + bi * 2048
        ldg = em.emit("gpsimd", lambda E, t0=tc, g0=greg: E.dma_start(
            out=arena[:, g0:g0 + 2048].rearrange("p (c e) -> p c e", c=NDC),
            in_=gfm[:, t0 * 128:(t0 + 1) * 128].rearrange("(c p) e -> p c e", p=128)),
            deps=[gfm_rd[bi]], mark=True, dma=True)
        for et in range(4):
            mm = None
            for d0 in range(NDC):
                mm = em.emit("tensor", lambda E, dd=d0, e0=et, g0=greg: E.matmul(
                    psO[:, 0:512], arena[:, g0 + dd * 128:g0 + (dd + 1) * 128],
                    arena[:, OW + dd * 2048 + e0 * 512: OW + dd * 2048 + e0 * 512 + 512],
                    start=(dd == 0), stop=(dd == NDC - 1)),
                    deps=([ldg, t_ow, psO_rd] if d0 == 0 else []), mark=(d0 == NDC - 1))
            ad = em.emit("vector", lambda E, x=xt, e0=et: E.tensor_add(
                x[:, e0 * 512:(e0 + 1) * 512], x[:, e0 * 512:(e0 + 1) * 512], psO[:, 0:512]),
                deps=[mm, ldx], mark=True)
            psO_rd = ad
        gfm_rd[bi] = mm
        stx = em.emit("sync", lambda E, x=xt, t0=tc: E.dma_start(
            out=x1s[t0 * 128:(t0 + 1) * 128, :], in_=x[:]), deps=[psO_rd], mark=True, dma=True)
        t_stats, t_apply = emit_ln_tile(em, xt[:], psO_rd, D, bufs, bufs["xln"][:],
                                        prev_apply=ln_prev, out_war=[last_tr])
        ln_prev = t_apply
        for d0 in range(NDC):
            slot = d0 % 2
            tr, evt = emit_transpose128(
                em, bufs["xln"][:, d0 * 128:(d0 + 1) * 128], consts, psT, slot,
                xlnT[:, d0 * TPC + tc * 128: d0 * TPC + (tc + 1) * 128],
                deps=[t_apply, slot_ev[slot], consts.get("ident_tok")])
            slot_tr[slot] = tr
            slot_ev[slot] = evt
        last_tr = slot_tr[(NDC - 1) % 2]
        xt_rd[bi] = [t_stats, t_apply, stx]
    xlnT_done = [slot_ev[0], slot_ev[1], ln_prev]

    cols = bufs["cols"]
    ncol, scol = cols[:, 6:7], cols[:, 7:8]
    psR_rd = None
    psG_rd = None
    selT_rd = None
    for tc in range(8):
        mm = None
        for d0 in range(NDC):
            mm = em.emit("tensor", lambda E, dd=d0, t0=tc: E.matmul(
                psR[:, 0:128], xlnT[:, dd * TPC + t0 * 128: dd * TPC + (t0 + 1) * 128],
                rwb[:, dd * 128:(dd + 1) * 128],
                start=(dd == 0), stop=(dd == NDC - 1)),
                deps=([t_rw, psR_rd] + xlnT_done if d0 == 0 else []), mark=(d0 == NDC - 1))
        r0 = em.emit("vector", lambda E: E.reduce_max(ncol, psR[:, 0:128], axis=AX.X,
                                                     negate=True), deps=[mm], mark=True)
        r1 = em.emit("scalar", lambda E: E.activation(probs[:], psR[:, 0:128], AF.Exp,
                                                      bias=ncol, accum_out=scol),
                     deps=[r0, mm], mark=True)
        psR_rd = r1
        r2 = em.emit("vector", lambda E: E.reciprocal(scol, scol), deps=[r1], mark=True)
        r3 = em.emit("vector", lambda E: E.tensor_scalar_mul(probs[:], probs[:], scol),
                     deps=[r2], mark=True)
        r4 = em.emit("vector", lambda E: E.max(out=m8[:], in_=probs[:]), deps=[r3], mark=True)
        r5 = em.emit("vector", lambda E: E.memset(m8[:, 4:8], -1.0), mark=True)
        r6 = em.emit("vector", lambda E: E.match_replace(out=zapped[:], in_to_replace=m8[:],
                                                         in_values=probs[:], imm_value=0.0),
                     deps=[r5], mark=True)
        r7 = em.emit("vector", lambda E: E.tensor_sub(zapped[:], probs[:], zapped[:]), mark=True)
        r8 = em.emit("vector", lambda E: E.reduce_sum(scol, zapped[:], axis=AX.X), mark=True)
        em.emit("vector", lambda E: E.tensor_scalar_add(scol, scol, MOE_EPS))
        r9 = em.emit("vector", lambda E, t0=tc: E.reciprocal(rcols[:, t0:t0 + 1], scol),
                     mark=True)
        if debug:
            em.emit("sync", lambda E, t0=tc: E.dma_start(out=dbg_probs[t0], in_=probs[:]),
                    deps=[r3], mark=True, dma=True)
        rc = em.emit("vector", lambda E: E.tensor_copy(selbf[:], zapped[:]), mark=True)
        trj, evj = emit_transpose128(em, selbf[:], consts, psT, 0, selT[:],
                                     deps=[rc, selT_rd, slot_ev[0]])
        ma = em.emit("tensor", lambda E: E.matmul(psG[:, 0:8], selT[:], mix2[:],
                                                  start=True, stop=True),
                     deps=[evj, t_mix, psG_rd], mark=True)
        selT_rd = ma
        aw = em.emit("vector", lambda E, t0=tc: E.tensor_scalar(
            alpha[:, t0 * 8:(t0 + 1) * 8], psG[:, 0:8], rcols[:, t0:t0 + 1], None,
            op0=ALU.mult), deps=[ma, r9], mark=True)
        psG_rd = aw
        slot_ev[0] = evj

    if debug:
        em.emit("sync", lambda E: E.dma_start(out=dbg_alpha, in_=alpha[:]),
                deps=[psG_rd], mark=True, dma=True)
        em.emit("sync", lambda E: E.dma_start(out=dbg_mix2, in_=mix2[:]),
                deps=[t_mix], mark=True, dma=True)
        em.emit("sync", lambda E: E.dma_start(out=dbg_xlnT, in_=xlnT[:]),
                deps=xlnT_done, mark=True, dma=True)

    wl_g, wl_u = None, None
    psU_rd = None
    comb_rd = None
    for et in range(16):
        k, ih = et // 2, et % 2
        wlg = em.emit("gpsimd", lambda E, e0=et: E.dma_start(
            out=arena[:, WG:WG + 8192].rearrange("p (c e) -> p c e", c=NDC),
            in_=wg[:, e0 * 512:(e0 + 1) * 512].rearrange("(c p) e -> p c e", p=128)),
            deps=[wl_g] + ([gfm_rd[0], gfm_rd[1], t_ow] if et == 0 else []),
            mark=True, dma=True)
        wlu = em.emit("gpsimd", lambda E, e0=et: E.dma_start(
            out=arena[:, WU:WU + 8192].rearrange("p (c e) -> p c e", c=NDC),
            in_=wu[:, e0 * 512:(e0 + 1) * 512].rearrange("(c p) e -> p c e", p=128)),
            deps=[wl_u], mark=True, dma=True)
        mmg_last, mmu_last = None, None
        for tc in range(8):
            mmg = None
            for d0 in range(NDC):
                mmg = em.emit("tensor", lambda E, dd=d0, t0=tc: E.matmul(
                    psG[:, 0:512], xlnT[:, dd * TPC + t0 * 128: dd * TPC + (t0 + 1) * 128],
                    arena[:, WG + dd * 512:WG + (dd + 1) * 512],
                    start=(dd == 0), stop=(dd == NDC - 1)),
                    deps=([wlg, psG_rd] if d0 == 0 else []), mark=(d0 == NDC - 1))
            mmu = None
            for d0 in range(NDC):
                mmu = em.emit("tensor", lambda E, dd=d0, t0=tc: E.matmul(
                    psU[:, 0:512], xlnT[:, dd * TPC + t0 * 128: dd * TPC + (t0 + 1) * 128],
                    arena[:, WU + dd * 512:WU + (dd + 1) * 512],
                    start=(dd == 0), stop=(dd == NDC - 1)),
                    deps=([wlu, psU_rd] if d0 == 0 else []), mark=(d0 == NDC - 1))
            sl = em.emit("scalar", lambda E: E.activation(sil[:], psG[:, 0:512], AF.Silu),
                         deps=[mmg], mark=True)
            psG_rd = sl
            bs = em.emit("vector", lambda E: E.tensor_mul(basis[:], sil[:], psU[:, 0:512]),
                         deps=[sl, mmu], mark=True)
            psU_rd = bs
            cview = combined[:, tc * 1024 + ih * 512: tc * 1024 + ih * 512 + 512]
            acol = alpha[:, tc * 8 + k: tc * 8 + k + 1]
            if k == 0:
                cw = em.emit("vector", lambda E, cv=cview, ac=acol: E.tensor_scalar_mul(
                    cv, basis[:], ac), deps=[bs], mark=True)
            else:
                cw = em.emit("vector", lambda E, cv=cview, ac=acol: E.scalar_tensor_tensor(
                    cv, basis[:], ac, cv, op0=ALU.mult, op1=ALU.add), deps=[bs], mark=True)
            comb_rd = cw
            mmg_last, mmu_last = mmg, mmu
        wl_g, wl_u = mmg_last, mmu_last

    if debug:
        em.emit("sync", lambda E: E.dma_start(out=dbg_comb, in_=combined[:]),
                deps=[comb_rd], mark=True, dma=True)

    swl = None
    prod_wr = {}
    for ic in range(16):
        wls = em.emit("gpsimd", lambda E, i0=ic: E.dma_start(
            out=arena[:, SW:SW + 2048].rearrange("p (c e) -> p c e", c=NDC),
            in_=swg[:, i0 * 128:(i0 + 1) * 128].rearrange("(c p) e -> p c e", p=128)),
            deps=[swl] + ([wl_g] if ic == 0 else []), mark=True, dma=True)
        for tt in range(2):
            mm = None
            for d0 in range(NDC):
                mm = em.emit("tensor", lambda E, dd=d0, t0=tt: E.matmul(
                    psG[:, 0:512], arena[:, SW + dd * 128:SW + (dd + 1) * 128],
                    xlnT[:, dd * TPC + t0 * 512: dd * TPC + t0 * 512 + 512],
                    start=(dd == 0), stop=(dd == NDC - 1)),
                    deps=(([wls, psG_rd] + ([wl_u, comb_rd] if ic == 0 and tt == 0 else []))
                          if d0 == 0 else []), mark=(d0 == NDC - 1))
            pv = arena[:, PR + ic * 1024 + tt * 512: PR + ic * 1024 + tt * 512 + 512]
            sg = em.emit("scalar", lambda E, p=pv: E.activation(p, psG[:, 0:512], AF.Silu),
                         deps=[mm], mark=True)
            psG_rd = sg
            prod_wr[(ic, tt)] = sg
        swl = mm
    for ic in range(16):
        wls = em.emit("gpsimd", lambda E, i0=ic: E.dma_start(
            out=arena[:, SW:SW + 2048].rearrange("p (c e) -> p c e", c=NDC),
            in_=swu[:, i0 * 128:(i0 + 1) * 128].rearrange("(c p) e -> p c e", p=128)),
            deps=[swl], mark=True, dma=True)
        for tt in range(2):
            mm = None
            for d0 in range(NDC):
                mm = em.emit("tensor", lambda E, dd=d0, t0=tt: E.matmul(
                    psU[:, 0:512], arena[:, SW + dd * 128:SW + (dd + 1) * 128],
                    xlnT[:, dd * TPC + t0 * 512: dd * TPC + t0 * 512 + 512],
                    start=(dd == 0), stop=(dd == NDC - 1)),
                    deps=([wls, psU_rd] if d0 == 0 else []), mark=(d0 == NDC - 1))
            pv = arena[:, PR + ic * 1024 + tt * 512: PR + ic * 1024 + tt * 512 + 512]
            pm = em.emit("vector", lambda E, p=pv: E.tensor_mul(p, p, psU[:, 0:512]),
                         deps=[mm, prod_wr[(ic, tt)]], mark=True)
            psU_rd = pm
            prod_wr[(ic, tt)] = pm
        swl = mm

    slot2 = [None, None]
    ct_done = []
    for tc in range(8):
        for ic in range(8):
            slot = (tc * 8 + ic) % 2
            cb = em.emit("vector", lambda E, t0=tc, i0=ic: E.tensor_copy(
                selbf[:], combined[:, t0 * 1024 + i0 * 128: t0 * 1024 + (i0 + 1) * 128]),
                deps=[comb_rd, slot2[slot]], mark=True)
            tr, evt = emit_transpose128(
                em, selbf[:], consts, psT, slot,
                arena[:, CT + ic * 1024 + tc * 128: CT + ic * 1024 + (tc + 1) * 128],
                deps=[cb, slot_ev[slot] if (tc == 0 and ic < 2) else None])
            slot2[slot] = evt
            ct_done.append(evt)
    ct_done = ct_done[-2:]

    wl_d, wl_s = None, None
    psO_rd2 = None
    xst_rd = None
    for et in range(4):
        wld = em.emit("gpsimd", lambda E, e0=et: E.dma_start(
            out=arena[:, WD:WD + 4096].rearrange("p (c e) -> p c e", c=8),
            in_=wd[:, e0 * 512:(e0 + 1) * 512].rearrange("(c p) e -> p c e", p=128)),
            deps=[wl_d, swl] + (ct_done if et == 0 else []), mark=True, dma=True)
        wlsd = em.emit("gpsimd", lambda E, e0=et: E.dma_start(
            out=arena[:, SWD:SWD + 8192].rearrange("p (c e) -> p c e", c=NDC),
            in_=swd[:, e0 * 512:(e0 + 1) * 512].rearrange("(c p) e -> p c e", p=128)),
            deps=[wl_s] + ([wl_g, wl_u] if et == 0 else []), mark=True, dma=True)
        mm_last = None
        for tc in range(8):
            mm = None
            for i0 in range(8):
                mm = em.emit("tensor", lambda E, ii=i0, t0=tc: E.matmul(
                    psO[:, 0:512], arena[:, CT + ii * 1024 + t0 * 128: CT + ii * 1024 + (t0 + 1) * 128],
                    arena[:, WD + ii * 512:WD + (ii + 1) * 512],
                    start=(ii == 0), stop=False),
                    deps=([wld, wlsd, psO_rd2] if i0 == 0 else []), mark=False)
            for i0 in range(16):
                mm = em.emit("tensor", lambda E, ii=i0, t0=tc: E.matmul(
                    psO[:, 0:512], arena[:, PR + ii * 1024 + t0 * 128: PR + ii * 1024 + (t0 + 1) * 128],
                    arena[:, SWD + ii * 512:SWD + (ii + 1) * 512],
                    start=False, stop=(ii == 15)), mark=(i0 == 15))
            ldx = em.emit("sync", lambda E, t0=tc, e0=et: E.dma_start(
                out=xst[:], in_=x1s[t0 * 128:(t0 + 1) * 128, e0 * 512:(e0 + 1) * 512]),
                deps=[xst_rd], mark=True, dma=True)
            ad = em.emit("vector", lambda E: E.tensor_add(xst[:], xst[:], psO[:, 0:512]),
                         deps=[mm, ldx], mark=True)
            psO_rd2 = ad
            sto = em.emit("sync", lambda E, t0=tc, e0=et: E.dma_start(
                out=x2o[t0 * 128:(t0 + 1) * 128, e0 * 512:(e0 + 1) * 512], in_=xst[:]),
                deps=[ad], mark=True, dma=True)
            xst_rd = sto
            mm_last = mm
        wl_d, wl_s = mm_last, mm_last


_progs = {}


def _get_prog(name, builder, **kw):
    if name not in _progs:
        _progs[name] = build_program(builder, **kw)
    return _progs[name]


def run_l0a(x_flat, qw_t, kw_t, vw_t):
    nc1 = _get_prog("l0a", build_l0a)
    in_maps = [{"xs": x_flat[c * TPC:(c + 1) * TPC],
                "qw": qw_t, "kw": kw_t, "vw": vw_t} for c in range(NCORES)]
    return run_bass_kernel_spmd(nc1, in_maps, core_ids=list(range(NCORES)))


def _retry(fn, keys, tag, tries=4):
    for t in range(tries):
        res = fn()
        ok = True
        for r in res.results:
            for k in keys:
                if not np.isfinite(np.asarray(r[k], dtype=np.float32)).all():
                    ok = False
                    break
            if not ok:
                break
        if ok:
            return res
        print(f"[kernel] launch {tag} produced non-finite values; retry {t + 1}")
    return res


def kernel(**inputs):
    g = {k: np.asarray(v) for k, v in inputs.items()}
    x = np.ascontiguousarray(g["L0_attn_x"].astype(np.float32, copy=False).reshape(B * S, D))

    r1 = _retry(lambda: run_l0a(x, np.ascontiguousarray(g["L0_attn_q_w"].T),
                                np.ascontiguousarray(g["L0_attn_k_w"].T),
                                np.ascontiguousarray(g["L0_attn_v_w"].T)),
                ["Qf", "Kf", "Vt"], "l0a")
    Qf_all = [np.concatenate([r1.results[c]["Qf"] for c in range(4 * b, 4 * b + 4)], 1)
              for b in range(2)]
    Kf_all = [np.concatenate([r1.results[c]["Kf"] for c in range(4 * b, 4 * b + 4)], 1)
              for b in range(2)]
    Vt_all = [np.concatenate([r1.results[c]["Vt"] for c in range(4 * b, 4 * b + 4)], 0)
              for b in range(2)]

    r2 = _retry(lambda: run_l0b(Qf_all, Kf_all, Vt_all,
                                np.ascontiguousarray(g["L0_attn_gate_w"].T),
                                np.ascontiguousarray(g["L0_attn_gate_b"].reshape(128, 1)),
                                make_mask_strip()), ["Oh"], "l0b")
    Oh_by_core = [r2.results[c]["Oh"] for c in range(NCORES)]

    r3 = _retry(lambda: run_l0c(x, Oh_by_core,
                 np.ascontiguousarray(g["L0_attn_o_w"].T),
                 np.ascontiguousarray(g["L0_ffn_w_router"]),
                 np.ascontiguousarray(g["L0_ffn_amplitudes"].reshape(128, 16)),
                 np.ascontiguousarray(g["L0_ffn_expert_scale"].reshape(128, 1)),
                 g["L0_ffn_w_gate"], g["L0_ffn_w_up"], g["L0_ffn_w_down"],
                 g["L0_ffn_shared_w_gate"], g["L0_ffn_shared_w_up"],
                 g["L0_ffn_shared_w_down"]), ["x2o"], "l0c")
    x2 = np.concatenate([r3.results[c]["x2o"] for c in range(NCORES)], 0)

    r4 = _retry(lambda: run_l1a(x2, np.ascontiguousarray(g["L1_attn_W_in"].T)),
                ["proj"], "l1a")
    proj_by_core = [r4.results[c]["proj"] for c in range(NCORES)]

    r5 = _retry(lambda: run_l1b(proj_by_core,
                                g["L1_attn_conv_w"].reshape(4352, 4),
                                g["L1_attn_conv_b"].reshape(4352, 1),
                                g["L1_attn_dt_bias"], g["L1_attn_A_log"],
                                np.repeat(g["L1_attn_D"], 64)), ["yf"], "l1b")
    yf_by_core = [r5.results[c]["yf"] for c in range(NCORES)]

    r6 = _retry(lambda: run_l1c(yf_by_core, x2,
                 np.ascontiguousarray(g["L1_attn_norm_w"].reshape(32, 128).T),
                 np.ascontiguousarray(g["L1_attn_W_out"].T),
                 np.ascontiguousarray(g["L1_ffn_Wr"].T),
                 np.ascontiguousarray(g["L1_ffn_Wg"].transpose(0, 2, 1)),
                 np.ascontiguousarray(g["L1_ffn_Wu"].transpose(0, 2, 1)),
                 np.ascontiguousarray(g["L1_ffn_Wd"].transpose(0, 2, 1)),
                 np.ascontiguousarray(g["L1_ffn_Wsg"].T),
                 np.ascontiguousarray(g["L1_ffn_Wsu"].T),
                 np.ascontiguousarray(g["L1_ffn_Wsd"].T)), ["outd"], "l1c")
    out = np.concatenate([r6.results[c]["outd"] for c in range(NCORES)], 0)
    return out.reshape(B, S, D).astype(np.float32)

def run_l0c(x_flat, Oh_by_core, ow_t, rw, amp2, escale, wg, wu, wd, swg, swu, swd):
    nc3 = _get_prog("l0c", build_l0c)
    in_maps = []
    for c in range(NCORES):
        b = c // 4
        trange = slice((c % 4) * TPC, (c % 4) * TPC + TPC)
        gfm = np.empty((D, TPC), dtype=BF)
        for h in range(NH0):
            gp = b * NH0 + h
            cp, j = gp // NPAIR, gp % NPAIR
            gfm[128 * h:128 * (h + 1), :] = Oh_by_core[cp][j][:, trange]
        in_maps.append({
            "xs": x_flat[c * TPC:(c + 1) * TPC], "gfm": gfm, "ow": ow_t,
            "rw": rw, "amp": amp2, "escale": escale, "wg": wg, "wu": wu,
            "wd": wd, "swg": swg, "swu": swu, "swd": swd,
        })
    return run_bass_kernel_spmd(nc3, in_maps, core_ids=list(range(NCORES)))


def build_l1a(em):
    nc = em.nc
    xs = nc.dram_tensor("xs", [TPC, D], F32, kind="ExternalInput").ap()
    win = nc.dram_tensor("win", [D, 8512], F32, kind="ExternalInput").ap()
    proj = nc.dram_tensor("proj", [8512, TPC], F32, kind="ExternalOutput").ap()

    consts = load_consts(em, need_ident=True)
    xlnT = em.sb("xlnT", [128, NDC * TPC], BF16)
    bufs = {
        "xtile": [em.sb(f"xtile{i}", [128, D], F32) for i in range(2)],
        "xln": em.sb("xln", [128, D], BF16),
        "xsq": em.sb("xsq", [128, D], F32),
        "cols": em.sb("cols", [128, 8], F32),
    }
    wbuf = [em.sb(f"wbuf{i}", [128, NDC * 128], BF16) for i in range(2)]
    stg = [em.sb(f"stg{i}", [128, 512], F32) for i in range(2)]
    psT = [em.ps(f"psT{i}", [128, 1024], BF16) for i in range(2)]
    psmm = [em.ps(f"psmm{i}", [128, 512], F32) for i in range(2)]

    xln_done = emit_lnT(em, consts, xs, TPC, xlnT, bufs, psT)

    wb_last = [None, None]
    st_last = [None, None]
    ps_last = [None, None]
    nchunks = 8512 // 128
    for e in range(67):
        cw = 128 if e < 66 else 64
        wb = wbuf[e % 2]
        wl = em.emit("gpsimd", lambda E, w=wb, e0=e, c0=cw: E.dma_start(
            out=w[:, 0:NDC * c0].rearrange("p (c e) -> p c e", c=NDC),
            in_=win[:, e0 * 128:e0 * 128 + c0].rearrange("(c p) e -> p c e", p=128)),
            deps=[wb_last[e % 2]] + xln_done, mark=True, dma=True)
        for tt in range(2):
            psu = psmm[tt]
            mm = None
            for d in range(NDC):
                mm = em.emit("tensor", lambda E, p=psu, w=wb, d0=d, t0=tt, c0=cw: E.matmul(
                    p[0:c0, 0:512], w[:, d0 * c0:(d0 + 1) * c0],
                    xlnT[:, d0 * TPC + t0 * 512: d0 * TPC + t0 * 512 + 512],
                    start=(d0 == 0), stop=(d0 == NDC - 1)),
                    deps=([wl, ps_last[tt]] if d == 0 else []), mark=(d == NDC - 1))
            so = stg[tt]
            ev = em.emit("vector", lambda E, o=so, p=psu, c0=cw: E.tensor_copy(
                o[0:c0, :], p[0:c0, 0:512]), deps=[mm, st_last[tt]], mark=True)
            ps_last[tt] = ev
            st = em.emit("sync", lambda E, o=so, e0=e, t0=tt, c0=cw: E.dma_start(
                out=proj[e0 * 128:e0 * 128 + c0, t0 * 512:(t0 + 1) * 512], in_=o[0:c0, :]),
                deps=[ev], mark=True, dma=True)
            st_last[tt] = st
        wb_last[e % 2] = mm


def run_l1a(x2_flat, win_t):
    nc4 = _get_prog("l1a", build_l1a)
    in_maps = [{"xs": x2_flat[c * TPC:(c + 1) * TPC], "win": win_t}
               for c in range(NCORES)]
    return run_bass_kernel_spmd(nc4, in_maps, core_ids=list(range(NCORES)))


NH_L1, PDIM, NDIM, CHUNK = 64, 64, 128, 256
HPC = 8


def build_l1b(em):
    nc = em.nc
    cin = nc.dram_tensor("cin", [2, 768, S], F32, kind="ExternalInput").ap()
    zin = nc.dram_tensor("zin", [2, 512, S], F32, kind="ExternalInput").ap()
    dtin = nc.dram_tensor("dtin", [2, 8, S], F32, kind="ExternalInput").ap()
    cw = nc.dram_tensor("cw", [768, 4], F32, kind="ExternalInput").ap()
    cb = nc.dram_tensor("cb", [768, 1], F32, kind="ExternalInput").ap()
    dtb = nc.dram_tensor("dtb", [8, 1], F32, kind="ExternalInput").ap()
    alog = nc.dram_tensor("alog", [8, 1], F32, kind="ExternalInput").ap()
    dexp = nc.dram_tensor("dexp", [512, 1], F32, kind="ExternalInput").ap()
    mstripd = nc.dram_tensor("mstrip", [128, 896], BF16, kind="ExternalInput").ap()
    yf = nc.dram_tensor("yf", [2, 512, S], F32, kind="ExternalOutput").ap()

    consts = load_consts(em, need_ident=True)
    identf_np = np.eye(128, dtype=np.float32)
    identf_d = nc.inline_tensor(identf_np, "c_identf")
    identf = em.sb("identf", [128, 128], F32)
    t_idf = em.emit("sync", lambda E: E.dma_start(out=identf[:], in_=identf_d.ap()),
                    mark=True, dma=True)
    sel_np = np.zeros((8, 8 * 128), dtype=np.float32)
    for h in range(8):
        sel_np[h, h * 128:(h + 1) * 128] = 1.0
    sel_d = nc.inline_tensor(sel_np, "c_sel8")
    sel8 = em.sb("sel8", [8, 8 * 128], F32)
    t_sel = em.emit("sync", lambda E: E.dma_start(out=sel8[:], in_=sel_d.ap()),
                    mark=True, dma=True)
    strip = em.sb("strip", [128, 896], BF16)
    t_strip = em.emit("sync", lambda E: E.dma_start(out=strip[:], in_=mstripd),
                      mark=True, dma=True)

    xsil = em.sb("xsil", [128, 6 * S], F32)
    Bbf = em.sb("Bbf", [128, S], BF16)
    Cbf = em.sb("Cbf", [128, S], BF16)
    scratch8 = em.sb("scratch8", [8, S], F32)
    cs8 = em.sb("cs8", [8, S], F32)
    ecs8 = em.sb("ecs8", [8, S], F32)
    tmp8 = em.sb("tmp8", [8, CHUNK], F32)
    convin = em.sb("convin", [128, S + 4], F32)
    cwsb = em.sb("cwsb", [128, 6 * 4], F32)
    cbsb = em.sb("cbsb", [128, 6], F32)
    dtbsb = em.sb("dtbsb", [8, 1], F32)
    alogsb = em.sb("alogsb", [8, 2], F32)
    Dsb = em.sb("Dsb", [128, 4], F32)
    dtT = em.sb("dtT", [128, 32 * 8], F32)
    negcsT = em.sb("negcsT", [128, 32 * 8], F32)
    wT = em.sb("wT", [128, 32 * 8], F32)
    Dch = em.sb("Dch", [8, 16], F32)
    Dcol = em.sb("Dcol", [128, 8 * 16], F32)
    x_tm = em.sb("x_tm", [128, 2 * 512], BF16)
    B_tm = em.sb("B_tm", [128, 2 * 128], BF16)
    Gm = em.sb("Gm", [128, 2 * CHUNK], BF16)
    castb = em.sb("castb", [128, 128], BF16)
    zsil = em.sb("zsil", [128, 4 * CHUNK], F32)
    seg_sb = em.sb("seg_sb", [128, CHUNK], F32)
    pexp = em.sb("pexp", [128, CHUNK], F32)
    Ptile = [em.sb(f"Ptile{i}", [128, CHUNK], BF16) for i in range(2)]
    Cp = em.sb("Cp", [128, CHUNK], BF16)
    Bw = [em.sb(f"Bw{i}", [128, 128], BF16) for i in range(2)]
    carry = em.sb("carry", [128, HPC * PDIM], F32)
    carrybf = em.sb("carrybf", [128, PDIM], BF16)
    y1f = em.sb("y1f", [128, CHUNK], F32)
    yfstf = em.sb("yfstf", [128, CHUNK], F32)

    psSel = em.ps("psSel", [128, 512], F32)
    psG = em.ps("psG", [128, 512], F32)
    psY = em.ps("psY", [128, 512], F32)
    psS = em.ps("psS", [128, 512], F32)
    psTb = [em.ps(f"psTb{i}", [128, 1024], BF16) for i in range(2)]
    psTf = em.ps("psTf", [128, 512], F32)

    t_cw = em.emit("sync", lambda E: E.dma_start(
        out=cwsb[:].rearrange("p (c e) -> p c e", c=6),
        in_=cw.rearrange("(c p) e -> p c e", p=128)), mark=True, dma=True)
    t_cb = em.emit("sync", lambda E: E.dma_start(
        out=cbsb[:].rearrange("p (c e) -> p c e", c=6),
        in_=cb.rearrange("(c p) e -> p c e", p=128),
        allow_slow_non_contiguous=True), mark=True, dma=True)
    t_dtb = em.emit("sync", lambda E: E.dma_start(out=dtbsb[:], in_=dtb), mark=True, dma=True)
    t_al = em.emit("sync", lambda E: E.dma_start(out=alogsb[:, 0:1], in_=alog), mark=True, dma=True)
    t_de = em.emit("sync", lambda E: E.dma_start(
        out=Dsb[:].rearrange("p (c e) -> p c e", c=4),
        in_=dexp.rearrange("(c p) e -> p c e", p=128),
        allow_slow_non_contiguous=True), mark=True, dma=True)

    a0 = em.emit("scalar", lambda E: E.activation(alogsb[:, 1:2], alogsb[:, 0:1], AF.Exp),
                 deps=[t_al], mark=True)
    t_acol = em.emit("vector", lambda E: E.tensor_scalar_mul(alogsb[:, 1:2], alogsb[:, 1:2],
                                                             -1.0), deps=[a0], mark=True)

    V, SC, T = "vector", "scalar", "tensor"
    war = {}

    for b in range(2):
        for ch in range(6):
            m0 = em.emit(V, lambda E: E.memset(convin[:, 0:3], 0.0),
                         deps=[war.get("convin")], mark=True)
            ld = em.emit("sync", lambda E, b0=b, c0=ch: E.dma_start(
                out=convin[:, 3:3 + S], in_=cin[b0, c0 * 128:(c0 + 1) * 128, :]),
                deps=[m0, war.get("convin")], mark=True, dma=True)
            acc = xsil[:, ch * S:(ch + 1) * S]
            tt0 = em.emit(V, lambda E, a=acc, c0=ch: E.tensor_scalar_mul(
                a, convin[:, 0:S], cwsb[:, c0 * 4:c0 * 4 + 1]),
                deps=[ld, t_cw, war.get(("xsil", ch))], mark=True)
            for j in range(1, 4):
                tt0 = em.emit(V, lambda E, a=acc, c0=ch, j0=j: E.scalar_tensor_tensor(
                    a, convin[:, j0:j0 + S], cwsb[:, c0 * 4 + j0:c0 * 4 + j0 + 1], a,
                    op0=ALU.mult, op1=ALU.add), mark=True)
            sl = em.emit(SC, lambda E, a=acc, c0=ch: E.activation(
                a, a, AF.Silu, bias=cbsb[:, c0:c0 + 1]), deps=[tt0, t_cb], mark=True)
            war[("xsil_w", ch)] = sl
            war["convin"] = tt0
        cB = em.emit(V, lambda E: E.tensor_copy(Bbf[:], xsil[:, 4 * S:5 * S]),
                     deps=[war[("xsil_w", 4)], war.get("Bbf")], mark=True)
        cC = em.emit(V, lambda E: E.tensor_copy(Cbf[:], xsil[:, 5 * S:6 * S]),
                     deps=[war[("xsil_w", 5)], war.get("Cbf")], mark=True)

        lddt = em.emit("sync", lambda E, b0=b: E.dma_start(out=scratch8[:], in_=dtin[b0]),
                       deps=war.get("dt_rd", []), mark=True, dma=True)
        d0 = em.emit(V, lambda E: E.tensor_scalar(scratch8[:], scratch8[:], dtbsb[:, 0:1],
                                                  None, op0=ALU.add),
                     deps=[lddt, t_dtb], mark=True)
        d1 = em.emit(SC, lambda E: E.activation(cs8[:], scratch8[:], AF.Abs),
                     deps=[d0, war.get("cs8_rd")], mark=True)
        d2 = em.emit(SC, lambda E: E.activation(cs8[:], cs8[:], AF.Exp, scale=-1.0), mark=True)
        d3 = em.emit(SC, lambda E: E.activation(cs8[:], cs8[:], AF.Ln, bias=1.0), mark=True)
        d4 = em.emit(SC, lambda E: E.activation(ecs8[:], scratch8[:], AF.Relu),
                     deps=[d0, war.get("ecs8_rd")], mark=True)
        d5 = em.emit(V, lambda E: E.tensor_add(scratch8[:], ecs8[:], cs8[:]),
                     deps=[d3, d4], mark=True)
        d6 = em.emit(V, lambda E: E.tensor_scalar_min(scratch8[:], scratch8[:], 1e10),
                     mark=True)
        d7 = em.emit(V, lambda E: E.tensor_scalar_mul(ecs8[:], scratch8[:], alogsb[:, 1:2]),
                     deps=[t_acol], mark=True)
        sc_t = None
        for c in range(16):
            sc_t = em.emit(V, lambda E, c0=c: E.tensor_tensor_scan(
                cs8[:, c0 * CHUNK:(c0 + 1) * CHUNK], ecs8[:, c0 * CHUNK:(c0 + 1) * CHUNK],
                ecs8[:, c0 * CHUNK:(c0 + 1) * CHUNK], 0.0,
                op0=ALU.add, op1=ALU.bypass), deps=[d7], mark=True)
        d8 = em.emit(SC, lambda E: E.activation(ecs8[:], cs8[:], AF.Exp),
                     deps=[sc_t], mark=True)
        d9 = em.emit(V, lambda E: E.tensor_copy(
            Dch[:].rearrange("p (c e) -> p c e", e=1),
            cs8[:].rearrange("p (c l) -> p c l", l=CHUNK)[:, :, CHUNK - 1:CHUNK]),
            deps=[sc_t, war.get("Dch_rd")], mark=True)
        d10 = em.emit(SC, lambda E: E.activation(Dch[:], Dch[:], AF.Exp), deps=[d9], mark=True)
        dcol_done = []
        psSel_rd = war.get("psSel_rd")
        for h in range(HPC):
            mmD = em.emit(T, lambda E, h0=h: E.matmul(
                psSel[:, 0:16], sel8[:, h0 * 128:(h0 + 1) * 128], Dch[:],
                start=True, stop=True), deps=[d10, t_sel, psSel_rd], mark=True)
            cpD = em.emit(V, lambda E, h0=h: E.tensor_copy(
                Dcol[:, h0 * 16:(h0 + 1) * 16], psSel[:, 0:16]),
                deps=[mmD, war.get("Dcol_rd")], mark=True)
            psSel_rd = cpD
            dcol_done.append(cpD)
        war["psSel_rd"] = psSel_rd

        tf_rd = war.get("psTf_rd")
        tT_done = []
        for c in range(16):
            w0 = em.emit(V, lambda E, c0=c: E.tensor_scalar(
                tmp8[:], cs8[:, c0 * CHUNK:(c0 + 1) * CHUNK],
                cs8[:, c0 * CHUNK + CHUNK - 1:c0 * CHUNK + CHUNK], -1.0,
                op0=ALU.subtract, op1=ALU.mult), deps=[sc_t], mark=True)
            w1 = em.emit(SC, lambda E: E.activation(tmp8[:], tmp8[:], AF.Exp),
                         deps=[w0], mark=True)
            w2 = em.emit(V, lambda E, c0=c: E.tensor_mul(
                tmp8[:], tmp8[:], scratch8[:, c0 * CHUNK:(c0 + 1) * CHUNK]),
                deps=[w1, d6], mark=True)
            for sub in range(2):
                tl = c * 2 + sub
                tr1 = em.emit(T, lambda E, c0=c, s0=sub: E.transpose(
                    psTf[:, 0:8], scratch8[:, c0 * CHUNK + s0 * 128:c0 * CHUNK + s0 * 128 + 128],
                    identf[0:8, 0:8]), deps=[d6, t_idf, tf_rd], mark=True)
                cp1 = em.emit(V, lambda E, t0=tl: E.tensor_copy(
                    dtT[:, t0 * 8:(t0 + 1) * 8], psTf[:, 0:8]),
                    deps=[tr1, war.get("dtT_rd")], mark=True)
                tr2 = em.emit(T, lambda E, c0=c, s0=sub: E.transpose(
                    psTf[:, 8:16], cs8[:, c0 * CHUNK + s0 * 128:c0 * CHUNK + s0 * 128 + 128],
                    identf[0:8, 0:8]), deps=[sc_t, cp1], mark=True)
                cp2 = em.emit(V, lambda E, t0=tl: E.tensor_scalar_mul(
                    negcsT[:, t0 * 8:(t0 + 1) * 8], psTf[:, 8:16], -1.0),
                    deps=[tr2, war.get("negcsT_rd")], mark=True)
                tr3 = em.emit(T, lambda E, s0=sub: E.transpose(
                    psTf[:, 16:24], tmp8[:, s0 * 128:s0 * 128 + 128],
                    identf[0:8, 0:8]), deps=[w2, cp2], mark=True)
                cp3 = em.emit(V, lambda E, t0=tl: E.tensor_copy(
                    wT[:, t0 * 8:(t0 + 1) * 8], psTf[:, 16:24]),
                    deps=[tr3, war.get("wT_rd")], mark=True)
                tf_rd = cp3
                tT_done.append(cp3)
        war["psTf_rd"] = tf_rd

        ci = em.emit(V, lambda E: E.memset(carry[:], 0.0),
                     deps=[war.get("carry_rd")], mark=True)

        ptile_rd = war.get("ptile_rd", [None, None])
        bw_rd = war.get("bw_rd", [None, None])
        psG_rd = war.get("psG_rd")
        psY_rd = war.get("psY_rd")
        psS_rd = war.get("psS_rd")
        psSel_rd = war["psSel_rd"]
        xtm_rd = war.get("xtm_rd")
        btm_rd = war.get("btm_rd")
        gm_rd = war.get("gm_rd")
        cp_rd = war.get("cp_rd")
        cbf_rd = war.get("cbf_rd")
        y1_rd = war.get("y1_rd")
        yfst_rd = war.get("yfst_rd")
        zsil_rd = war.get("zsil_rd")
        castb_rd = war.get("castb_rd")
        tb_rd = war.get("tb_rd", [None, None])
        for c in range(16):
            lz = em.emit("scalar", lambda E, b0=b, c0=c: E.dma_start(
                out=zsil[:].rearrange("p (zc t) -> p zc t", zc=4),
                in_=zin[b0][:, c0 * CHUNK:(c0 + 1) * CHUNK]
                .rearrange("(zc p) t -> p zc t", p=128)),
                deps=[zsil_rd], mark=True, dma=True)
            zs = None
            for zc in range(4):
                zs = em.emit(SC, lambda E, z0=zc: E.activation(
                    zsil[:, z0 * CHUNK:(z0 + 1) * CHUNK],
                    zsil[:, z0 * CHUNK:(z0 + 1) * CHUNK], AF.Silu),
                    deps=[lz], mark=True)
            xtm_toks = []
            for sub in range(2):
                for xc in range(4):
                    slot = (sub * 4 + xc) % 2
                    cc = em.emit(V, lambda E, x0=xc, c0=c, s0=sub: E.tensor_copy(
                        castb[:], xsil[:, x0 * S + c0 * CHUNK + s0 * 128:
                                       x0 * S + c0 * CHUNK + s0 * 128 + 128]),
                        deps=[war[("xsil_w", xc)], castb_rd], mark=True)
                    trx = em.emit(T, lambda E, p=psTb[slot]: E.transpose(
                        p[:, 0:128], castb[:], consts["ident"][:]),
                        deps=[cc, tb_rd[slot], consts["ident_tok"]], mark=True)
                    cpx = em.emit(V, lambda E, p=psTb[slot], x0=xc, s0=sub: E.tensor_copy(
                        x_tm[:, s0 * 512 + x0 * 128:s0 * 512 + (x0 + 1) * 128], p[:, 0:128]),
                        deps=[trx, xtm_rd], mark=True)
                    castb_rd = trx
                    tb_rd[slot] = cpx
                    xtm_toks.append(cpx)
                trB = em.emit(T, lambda E, p=psTb[0], c0=c, s0=sub: E.transpose(
                    p[:, 128:256], Bbf[:, c0 * CHUNK + s0 * 128:c0 * CHUNK + s0 * 128 + 128],
                    consts["ident"][:]), deps=[cB, tb_rd[0]], mark=True)
                cpB = em.emit(V, lambda E, p=psTb[0], s0=sub: E.tensor_copy(
                    B_tm[:, s0 * 128:(s0 + 1) * 128], p[:, 128:256]),
                    deps=[trB, btm_rd], mark=True)
                tb_rd[0] = cpB
                xtm_toks.append(cpB)
            xtm_done = xtm_toks[-1]
            btm_done = cpB
            gtoks = []
            for sub in range(2):
                mmG = em.emit(T, lambda E, c0=c, s0=sub: E.matmul(
                    psG[:, 0:CHUNK], Bbf[:, c0 * CHUNK + s0 * 128:c0 * CHUNK + s0 * 128 + 128],
                    Cbf[:, c0 * CHUNK:(c0 + 1) * CHUNK], start=True, stop=True),
                    deps=[cB, cC, psG_rd], mark=True)
                mk = em.emit(V, lambda E, s0=sub: E.tensor_mul(
                    Gm[:, s0 * CHUNK:(s0 + 1) * CHUNK], psG[:, 0:CHUNK],
                    strip[:, 384 - s0 * 128:384 - s0 * 128 + CHUNK]),
                    deps=[mmG, t_strip, gm_rd], mark=True)
                psG_rd = mk
                gtoks.append(mk)
            for q in range(4):
                pair_mmY = []
                for h in (2 * q, 2 * q + 1):
                    odd = h % 2
                    pt_toks = []
                    for sub in range(2):
                        tl = c * 2 + sub
                        mmB = em.emit(T, lambda E, h0=h, c0=c: E.matmul(
                            psSel[:, 0:CHUNK], sel8[:, h0 * 128:(h0 + 1) * 128],
                            cs8[:, c0 * CHUNK:(c0 + 1) * CHUNK], start=True, stop=True),
                            deps=[sc_t, t_sel, psSel_rd], mark=True)
                        sg = em.emit(V, lambda E, t0=tl, h0=h: E.tensor_scalar(
                            seg_sb[:], psSel[:, 0:CHUNK],
                            negcsT[:, t0 * 8 + h0:t0 * 8 + h0 + 1], 0.0,
                            op0=ALU.add, op1=ALU.min),
                            deps=[mmB] + tT_done[-1:], mark=True)
                        psSel_rd = sg
                        ex = em.emit(SC, lambda E: E.activation(pexp[:], seg_sb[:], AF.Exp),
                                     deps=[sg], mark=True)
                        pt = em.emit(V, lambda E, t0=tl, h0=h, s0=sub, pi=sub: E.scalar_tensor_tensor(
                            Ptile[pi][:], pexp[:], dtT[:, t0 * 8 + h0:t0 * 8 + h0 + 1],
                            Gm[:, s0 * CHUNK:(s0 + 1) * CHUNK], op0=ALU.mult, op1=ALU.mult),
                            deps=[ex, ptile_rd[sub], gtoks[sub]], mark=True)
                        pt_toks.append(pt)
                    mmE = em.emit(T, lambda E, h0=h, c0=c: E.matmul(
                        psSel[:, 256:256 + CHUNK], sel8[:, h0 * 128:(h0 + 1) * 128],
                        ecs8[:, c0 * CHUNK:(c0 + 1) * CHUNK], start=True, stop=True),
                        deps=[d8, psSel_rd], mark=True)
                    cpc = em.emit(V, lambda E, c0=c: E.tensor_mul(
                        Cp[:], Cbf[:, c0 * CHUNK:(c0 + 1) * CHUNK], psSel[:, 256:256 + CHUNK]),
                        deps=[mmE, cC, cp_rd], mark=True)
                    psSel_rd = cpc
                    cbf = em.emit(V, lambda E, h0=h: E.tensor_copy(
                        carrybf[:], carry[:, h0 * PDIM:(h0 + 1) * PDIM]),
                        deps=[ci, cbf_rd], mark=True)
                    pslice = psY[64:128, 0:CHUNK] if odd else psY[0:64, 0:CHUNK]
                    tpos = (0, 64) if odd else (0, 0)
                    mmY = []
                    for sub in range(2):
                        mmY.append(em.emit(T, lambda E, h0=h, s0=sub, pi=sub, pv=pslice, tp=tpos:
                                           E.matmul(
                            pv, x_tm[:, s0 * 512 + h0 * 64:s0 * 512 + (h0 + 1) * 64],
                            Ptile[pi][:], start=(s0 == 0), stop=False, tile_position=tp),
                            deps=[pt_toks[sub], xtm_done] + ([psY_rd] if (sub == 0 and not odd) else []),
                            mark=True))
                    mmY.append(em.emit(T, lambda E, pv=pslice, tp=tpos: E.matmul(
                        pv, carrybf[:, 0:PDIM], Cp[:],
                        start=False, stop=True, tile_position=tp), deps=[cbf, cpc], mark=True))
                    pair_mmY.append(mmY[2])
                    ptile_rd = [mmY[0], mmY[1]]
                    cp_rd = mmY[2]
                    st_toks = []
                    for sub in range(2):
                        tl = c * 2 + sub
                        bw = em.emit(V, lambda E, t0=tl, h0=h, s0=sub, pi=sub: E.tensor_scalar_mul(
                            Bw[pi][:], B_tm[:, s0 * 128:(s0 + 1) * 128],
                            wT[:, t0 * 8 + h0:t0 * 8 + h0 + 1]),
                            deps=[btm_done, bw_rd[sub]] + tT_done[-1:], mark=True)
                        mmS = em.emit(T, lambda E, h0=h, s0=sub, pi=sub: E.matmul(
                            psS[:, 0:PDIM], Bw[pi][:],
                            x_tm[:, s0 * 512 + h0 * 64:s0 * 512 + (h0 + 1) * 64],
                            start=(s0 == 0), stop=(s0 == 1)),
                            deps=[bw, xtm_done] + ([psS_rd] if sub == 0 else []), mark=True)
                        st_toks.append(mmS)
                        bw_rd[sub] = mmS
                    cu = em.emit(V, lambda E, h0=h, c0=c: E.scalar_tensor_tensor(
                        carry[:, h0 * PDIM:(h0 + 1) * PDIM],
                        carry[:, h0 * PDIM:(h0 + 1) * PDIM],
                        Dcol[:, h0 * 16 + c0:h0 * 16 + c0 + 1], psS[:, 0:PDIM],
                        op0=ALU.mult, op1=ALU.add),
                        deps=[st_toks[-1], cbf] + dcol_done[-1:], mark=True)
                    psS_rd = cu
                    cbf_rd = cu
                yd = em.emit(V, lambda E, q0=q, c0=c: E.scalar_tensor_tensor(
                    y1f[:], xsil[:, q0 * S + c0 * CHUNK:q0 * S + (c0 + 1) * CHUNK],
                    Dsb[:, q0:q0 + 1], psY[:, 0:CHUNK],
                    op0=ALU.mult, op1=ALU.add),
                    deps=pair_mmY + [t_de, y1_rd], mark=True)
                psY_rd = yd
                ym = em.emit(V, lambda E, q0=q: E.tensor_mul(
                    yfstf[:], y1f[:], zsil[:, q0 * CHUNK:(q0 + 1) * CHUNK]),
                    deps=[yd, zs, yfst_rd], mark=True)
                y1_rd = ym
                sty = em.emit("sync", lambda E, b0=b, q0=q, c0=c: E.dma_start(
                    out=yf[b0, q0 * 128:(q0 + 1) * 128, c0 * CHUNK:(c0 + 1) * CHUNK],
                    in_=yfstf[:]), deps=[ym], mark=True, dma=True)
                yfst_rd = sty
            zsil_rd = ym
            xtm_rd = st_toks[-1]
            btm_rd = st_toks[-1]
            gm_rd = pt_toks[-1]
            cbf_rd = cu
        war["ptile_rd"] = ptile_rd
        war["bw_rd"] = bw_rd
        war["psG_rd"] = psG_rd
        war["psY_rd"] = psY_rd
        war["psS_rd"] = psS_rd
        war["psSel_rd"] = psSel_rd
        war["xtm_rd"] = xtm_rd
        war["btm_rd"] = btm_rd
        war["gm_rd"] = gm_rd
        war["cp_rd"] = cp_rd
        war["cbf_rd"] = cbf_rd
        war["y1_rd"] = y1_rd
        war["yfst_rd"] = yfst_rd
        war["zsil_rd"] = zsil_rd
        war["castb_rd"] = castb_rd
        war["tb_rd"] = tb_rd
        war["carry_rd"] = cu
        war["Dch_rd"] = dcol_done[-1]
        war["Dcol_rd"] = cu
        war["dtT_rd"] = pt_toks[-1]
        war["negcsT_rd"] = psSel_rd
        war["wT_rd"] = bw_rd[1]
        war["cs8_rd"] = mmB
        war["ecs8_rd"] = mmE
        war["dt_rd"] = [d6, d7, w2]
        war["Bbf"] = mmG
        war["Cbf"] = cpc
        for ch in range(6):
            war[("xsil", ch)] = yd if ch < 4 else (mmG if ch == 4 else cpc)


def run_l1b(proj_by_core, cw, cb_, dtb, alog, dexp):
    nc5 = _get_prog("l1b", build_l1b)
    in_maps = []
    for c in range(NCORES):
        cin = np.empty((2, 768, S), dtype=np.float32)
        zin = np.empty((2, 512, S), dtype=np.float32)
        dtin = np.empty((2, 8, S), dtype=np.float32)
        for b2 in range(2):
            cores = range(4 * b2, 4 * b2 + 4)
            xrows = slice(4096 + 512 * c, 4096 + 512 * (c + 1))
            bc_rows = slice(8192, 8448)
            zrows = slice(512 * c, 512 * (c + 1))
            dtrows = slice(8448 + 8 * c, 8448 + 8 * (c + 1))
            cin[b2, 0:512] = np.concatenate([proj_by_core[cc][xrows] for cc in cores], 1)
            cin[b2, 512:768] = np.concatenate([proj_by_core[cc][bc_rows] for cc in cores], 1)
            zin[b2] = np.concatenate([proj_by_core[cc][zrows] for cc in cores], 1)
            dtin[b2] = np.concatenate([proj_by_core[cc][dtrows] for cc in cores], 1)
        cwm = np.concatenate([cw[512 * c:512 * (c + 1)], cw[4096:4352]], 0)
        cbm = np.concatenate([cb_[512 * c:512 * (c + 1)], cb_[4096:4352]], 0)
        in_maps.append({
            "cin": cin, "zin": zin, "dtin": dtin,
            "cw": np.ascontiguousarray(cwm.reshape(768, 4)),
            "cb": np.ascontiguousarray(cbm.reshape(768, 1)),
            "dtb": np.ascontiguousarray(dtb[8 * c:8 * (c + 1)].reshape(8, 1)),
            "alog": np.ascontiguousarray(alog[8 * c:8 * (c + 1)].reshape(8, 1)),
            "dexp": np.ascontiguousarray(dexp[512 * c:512 * (c + 1)].reshape(512, 1)),
            "mstrip": make_mask_strip(),
        })
    return run_bass_kernel_spmd(nc5, in_maps, core_ids=list(range(NCORES)))


def build_l1c(em, debug=False):
    nc = em.nc
    yfc = nc.dram_tensor("yfc", [4096, TPC], F32, kind="ExternalInput").ap()
    xs = nc.dram_tensor("xs", [TPC, D], F32, kind="ExternalInput").ap()
    nw = nc.dram_tensor("nw", [128, 32], F32, kind="ExternalInput").ap()
    wo = nc.dram_tensor("wo", [4096, D], F32, kind="ExternalInput").ap()
    wr = nc.dram_tensor("wr", [D, 8], F32, kind="ExternalInput").ap()
    wgt = nc.dram_tensor("wgt", [8, D, 1408], F32, kind="ExternalInput").ap()
    wut = nc.dram_tensor("wut", [8, D, 1408], F32, kind="ExternalInput").ap()
    wdt = nc.dram_tensor("wdt", [8, 1408, D], F32, kind="ExternalInput").ap()
    wsg = nc.dram_tensor("wsg", [D, 2816], F32, kind="ExternalInput").ap()
    wsu = nc.dram_tensor("wsu", [D, 2816], F32, kind="ExternalInput").ap()
    wsd = nc.dram_tensor("wsd", [2816, D], F32, kind="ExternalInput").ap()
    outd = nc.dram_tensor("outd", [TPC, D], F32, kind="ExternalOutput").ap()
    kx = "ExternalOutput" if debug else "Internal"
    x3s = nc.dram_tensor("x3s", [TPC, D], F32, kind=kx).ap()
    ynS = nc.dram_tensor("ynS", [4096, TPC], BF16, kind=kx).ap()
    if debug:
        dbg_wfull = nc.dram_tensor("dbg_wfull", [128, 64], F32, kind="ExternalOutput").ap()
        dbg_acc0 = nc.dram_tensor("dbg_acc0", [TPC, D], F32, kind="ExternalOutput").ap()
        dbg_rstd = nc.dram_tensor("dbg_rstd", [128, 1024], F32, kind="ExternalOutput").ap()
        dbg_rrow = nc.dram_tensor("dbg_rrow", [1, 1024], F32, kind="ExternalOutput").ap()
        dbg_sums = nc.dram_tensor("dbg_sums", [1, 1024], F32, kind="ExternalOutput").ap()

    consts = load_consts(em, need_ident=True, need_ones=True)
    ones_c, ones_r = consts["ones_col"], consts["ones_row"]

    acc = em.sb("acc", [128, 8 * D], F32)
    xlnT3 = em.sb("xlnT3", [128, NDC * TPC], BF16)
    arena = em.sb("arena", [128, 28160], BF16)
    bufs = {
        "xtile": [em.sb(f"xtile{i}", [128, D], F32) for i in range(2)],
        "xln": em.sb("xln", [128, D], BF16),
        "xsq": em.sb("xsq", [128, D], F32),
        "cols": em.sb("cols", [128, 8], F32),
    }
    sqb = [arena[:, 2048:3072], arena[:, 3072:4096]]
    yst32 = [em.sb(f"yst32_{i}", [128, 1024], F32) for i in range(2)]
    rrow = em.sb("rrow", [1, 1024], F32)
    rrow_bf = em.sb("rrow_bf", [1, 1024], BF16)
    rstdbc = em.sb("rstdbc", [128, 1024], F32)
    nwsb = em.sb("nwsb", [128, 32], F32)
    wrb = em.sb("wrb", [128, NDC * 8], BF16)
    probs8 = em.sb("probs8", [128, 8], F32)
    zap8 = em.sb("zap8", [128, 8], F32)
    m8 = em.sb("m8", [128, 8], F32)
    wfull = em.sb("wfull", [128, 8 * 8], F32)
    sil = em.sb("sil", [128, 512], F32)

    psT = [em.ps(f"psT{i}", [128, 1024], BF16) for i in range(2)]
    psA = em.ps("psA", [128, 512], F32)
    psB = em.ps("psB", [128, 512], F32)
    psC = em.ps("psC", [128, 512], F32)
    psD = em.ps("psD", [128, 512], F32)

    t_nw = em.emit("sync", lambda E: E.dma_start(out=nwsb[:], in_=nw), mark=True, dma=True)
    t_wr = em.emit("gpsimd", lambda E: E.dma_start(
        out=wrb[:].rearrange("p (c e) -> p c e", c=NDC),
        in_=wr.rearrange("(c p) e -> p c e", p=128)), mark=True, dma=True)

    ssq = em.sb("ssq", [128, 1024], F32)
    sqtmp = rstdbc
    stg = [yst32[0][:, 0:512], yst32[1][:, 0:512]]
    ld_rd = [None, None]
    for ch in range(32):
        bi = ch % 2
        ld = em.emit("sync", lambda E, c0=ch, s_=yst32[bi]: E.dma_start(
            out=s_[:], in_=yfc[c0 * 128:(c0 + 1) * 128, :]),
            deps=[ld_rd[bi]], mark=True, dma=True)
        if ch == 0:
            sq = em.emit("vector", lambda E, s_=yst32[bi]: E.tensor_mul(
                ssq[:], s_[:], s_[:]), deps=[ld], mark=True)
        else:
            em.emit("vector", lambda E, s_=yst32[bi]: E.tensor_mul(
                sqtmp[:], s_[:], s_[:]), deps=[ld], mark=True)
            sq = em.emit("vector", lambda E: E.tensor_add(ssq[:], ssq[:], sqtmp[:]),
                         mark=True)
        ld_rd[bi] = sq
    sqc = em.emit("vector", lambda E: E.tensor_copy(sqb[0], ssq[:]), mark=True)
    mm_rd = [None, None]
    for hh in range(2):
        psu = psA if hh == 0 else psB
        mm_rd[hh] = em.emit("tensor", lambda E, p=psu, h0=hh: E.matmul(
            p[0:1, 0:512], ones_c[:], arena[:, 2048 + h0 * 512:2048 + (h0 + 1) * 512],
            start=True, stop=True),
            deps=[sqc, consts["ones_tok"]], mark=True)
    c0 = em.emit("vector", lambda E: E.tensor_copy(rrow[0:1, 0:512], psA[0:1, 0:512]),
                 deps=[mm_rd[0]], mark=True)
    c1 = em.emit("vector", lambda E: E.tensor_copy(rrow[0:1, 512:1024], psB[0:1, 0:512]),
                 deps=[mm_rd[1]], mark=True)
    if debug:
        em.emit("sync", lambda E: E.dma_start(out=dbg_sums, in_=rrow[0:1, :]),
                deps=[c0, c1], mark=True, dma=True)
    me = em.emit("vector", lambda E: E.tensor_scalar(rrow[0:1, :], rrow[0:1, :], 1.0 / 4096.0,
                                                     1e-5, op0=ALU.mult, op1=ALU.add),
                 mark=True)
    r0 = me
    rb = em.emit("vector", lambda E: E.tensor_copy(rrow_bf[0:1, :], rrow[0:1, :]),
                 deps=[me], mark=True)
    bc_toks = []
    for hh in range(2):
        psu = psA if hh == 0 else psB
        mmb = em.emit("tensor", lambda E, p=psu, h0=hh: E.matmul(
            p[:, 0:512], ones_r[:], rrow_bf[0:1, h0 * 512:(h0 + 1) * 512],
            start=True, stop=True), deps=[rb, c0, c1], mark=True)
        sq2 = em.emit("scalar", lambda E, p=psu, h0=hh: E.activation(
            rstdbc[:, h0 * 512:(h0 + 1) * 512], p[:, 0:512], AF.Sqrt),
            deps=[mmb], mark=True)
        cpb = em.emit("vector", lambda E, h0=hh: E.reciprocal(
            rstdbc[:, h0 * 512:(h0 + 1) * 512], rstdbc[:, h0 * 512:(h0 + 1) * 512]),
            deps=[sq2], mark=True)
        bc_toks.append(cpb)

    if debug:
        em.emit("sync", lambda E: E.dma_start(out=dbg_rstd, in_=rstdbc[:]),
                deps=bc_toks, mark=True, dma=True)
        em.emit("sync", lambda E: E.dma_start(out=dbg_rrow, in_=rrow[0:1, :]),
                deps=[r0], mark=True, dma=True)

    YST = [0, 1024]
    for ch in range(32):
        bi = ch % 2
        reg = YST[bi]
        ld = em.emit("sync", lambda E, c0=ch, s_=yst32[bi]: E.dma_start(
            out=s_[:], in_=yfc[c0 * 128:(c0 + 1) * 128, :]),
            deps=[ld_rd[bi]], mark=True, dma=True)
        ap = em.emit("vector", lambda E, c0=ch, r0=reg, s_=yst32[bi]: E.scalar_tensor_tensor(
            arena[:, r0:r0 + 1024], s_[:], nwsb[:, c0:c0 + 1],
            rstdbc[:], op0=ALU.mult, op1=ALU.mult),
            deps=[ld, t_nw, ld_rd[bi]] + bc_toks, mark=True)
        st = em.emit("sync", lambda E, c0=ch, r0=reg: E.dma_start(
            out=ynS[c0 * 128:(c0 + 1) * 128, :], in_=arena[:, r0:r0 + 1024]),
            deps=[ap], mark=True, dma=True)
        ld_rd[bi] = [sq for sq in ()] or ap
        ld_rd[bi] = ap

    WOH = [2048, 10240]
    YNR = [18432, 22528]
    wo_rd = [None, None]
    yn_rd = [None, None]
    ps_rd = None
    st_rd = [None, None]
    for et in range(8):
        bi = et % 2
        wl = em.emit("gpsimd", lambda E, e0=et, r0=WOH[bi]: E.dma_start(
            out=arena[:, r0:r0 + 8192].rearrange("p (c e) -> p c e", c=32),
            in_=wo[:, e0 * 256:(e0 + 1) * 256].rearrange("(c p) e -> p c e", p=128)),
            deps=[wo_rd[bi]] + ([ld_rd[0], ld_rd[1]] if et < 2 else []), mark=True, dma=True)
        for tc in range(8):
            ybi = tc % 2
            yl = em.emit("scalar", lambda E, t0=tc, r0=YNR[ybi]: E.dma_start(
                out=arena[:, r0:r0 + 4096].rearrange("p (c e) -> p c e", c=32),
                in_=ynS[:, t0 * 128:(t0 + 1) * 128].rearrange("(c p) e -> p c e", p=128)),
                deps=[yn_rd[ybi]] + ([ld_rd[0], ld_rd[1]] if (et == 0 and tc < 2) else []),
                mark=True, dma=True)
            mm = None
            for d0 in range(32):
                mm = em.emit("tensor", lambda E, dd=d0, r0=WOH[bi], y0=YNR[ybi]: E.matmul(
                    psC[:, 0:256], arena[:, y0 + dd * 128:y0 + (dd + 1) * 128],
                    arena[:, r0 + dd * 256:r0 + (dd + 1) * 256],
                    start=(dd == 0), stop=(dd == 31)),
                    deps=([wl, yl, ps_rd] if d0 == 0 else []), mark=(d0 == 31))
            yn_rd[ybi] = mm
            sg = yst32[ybi]
            lx = em.emit("sync", lambda E, t0=tc, e0=et, s=sg: E.dma_start(
                out=s[:, 0:256], in_=xs[t0 * 128:(t0 + 1) * 128, e0 * 256:(e0 + 1) * 256]),
                deps=[st_rd[ybi]], mark=True, dma=True)
            ad = em.emit("vector", lambda E, s=sg: E.tensor_add(s[:, 0:256], s[:, 0:256],
                                                                psC[:, 0:256]),
                         deps=[mm, lx], mark=True)
            ps_rd = ad
            st = em.emit("sync", lambda E, t0=tc, e0=et, s=sg: E.dma_start(
                out=x3s[t0 * 128:(t0 + 1) * 128, e0 * 256:(e0 + 1) * 256], in_=s[:, 0:256]),
                deps=[ad], mark=True, dma=True)
            st_rd[ybi] = st
        wo_rd[bi] = mm

    ln_done = emit_lnT(em, consts, x3s, TPC, xlnT3, bufs, psT,
                       first_deps=[st_rd[0], st_rd[1]])

    psR_rd = None
    for tc in range(8):
        mm = None
        for d0 in range(NDC):
            mm = em.emit("tensor", lambda E, dd=d0, t0=tc: E.matmul(
                psD[:, 0:8], xlnT3[:, dd * TPC + t0 * 128: dd * TPC + (t0 + 1) * 128],
                wrb[:, dd * 8:(dd + 1) * 8],
                start=(dd == 0), stop=(dd == NDC - 1)),
                deps=([t_wr, psR_rd] + ln_done if d0 == 0 else []), mark=(d0 == NDC - 1))
        cols = bufs["cols"]
        ncol, scol = cols[:, 6:7], cols[:, 7:8]
        r1 = em.emit("vector", lambda E: E.reduce_max(ncol, psD[:, 0:8], axis=AX.X,
                                                      negate=True), deps=[mm], mark=True)
        r2 = em.emit("scalar", lambda E: E.activation(probs8[:], psD[:, 0:8], AF.Exp,
                                                      bias=ncol, accum_out=scol),
                     deps=[r1, mm], mark=True)
        psR_rd = r2
        r3 = em.emit("vector", lambda E: E.reciprocal(scol, scol), deps=[r2], mark=True)
        r4 = em.emit("vector", lambda E: E.tensor_scalar_mul(probs8[:], probs8[:], scol),
                     deps=[r3], mark=True)
        r5 = em.emit("vector", lambda E: E.max(out=m8[:], in_=probs8[:]), mark=True)
        r6 = em.emit("vector", lambda E: E.memset(m8[:, 2:8], -1.0), mark=True)
        r7 = em.emit("vector", lambda E: E.match_replace(out=zap8[:], in_to_replace=m8[:],
                                                         in_values=probs8[:], imm_value=0.0),
                     deps=[r6], mark=True)
        em.emit("vector", lambda E, t0=tc: E.tensor_sub(wfull[:, t0 * 8:(t0 + 1) * 8],
                                                        probs8[:], zap8[:]))

    PRS = 0
    SCL = 22528
    scl_rd = None
    psG_rd = ps_rd
    psU_rd = None
    prod_wr = {}
    for ic in range(22):
        wls = em.emit("gpsimd", lambda E, i0=ic: E.dma_start(
            out=arena[:, SCL:SCL + 2048].rearrange("p (c e) -> p c e", c=NDC),
            in_=wsg[:, i0 * 128:(i0 + 1) * 128].rearrange("(c p) e -> p c e", p=128)),
            deps=[scl_rd] + ([wo_rd[0], wo_rd[1], ld_rd[0], ld_rd[1], yn_rd[0], yn_rd[1]]
                             if ic == 0 else []), mark=True, dma=True)
        for tt in range(2):
            mm = None
            for d0 in range(NDC):
                mm = em.emit("tensor", lambda E, dd=d0, t0=tt: E.matmul(
                    psA[:, 0:512], arena[:, SCL + dd * 128:SCL + (dd + 1) * 128],
                    xlnT3[:, dd * TPC + t0 * 512: dd * TPC + t0 * 512 + 512],
                    start=(dd == 0), stop=(dd == NDC - 1)),
                    deps=([wls, psG_rd] if d0 == 0 else []), mark=(d0 == NDC - 1))
            pv = arena[:, PRS + ic * 1024 + tt * 512: PRS + ic * 1024 + tt * 512 + 512]
            sgt = em.emit("scalar", lambda E, p=pv: E.activation(p, psA[:, 0:512], AF.Silu),
                          deps=[mm], mark=True)
            psG_rd = sgt
            prod_wr[(ic, tt)] = sgt
        scl_rd = mm
    for ic in range(22):
        wls = em.emit("gpsimd", lambda E, i0=ic: E.dma_start(
            out=arena[:, SCL:SCL + 2048].rearrange("p (c e) -> p c e", c=NDC),
            in_=wsu[:, i0 * 128:(i0 + 1) * 128].rearrange("(c p) e -> p c e", p=128)),
            deps=[scl_rd], mark=True, dma=True)
        for tt in range(2):
            mm = None
            for d0 in range(NDC):
                mm = em.emit("tensor", lambda E, dd=d0, t0=tt: E.matmul(
                    psB[:, 0:512], arena[:, SCL + dd * 128:SCL + (dd + 1) * 128],
                    xlnT3[:, dd * TPC + t0 * 512: dd * TPC + t0 * 512 + 512],
                    start=(dd == 0), stop=(dd == NDC - 1)),
                    deps=([wls, psU_rd] if d0 == 0 else []), mark=(d0 == NDC - 1))
            pv = arena[:, PRS + ic * 1024 + tt * 512: PRS + ic * 1024 + tt * 512 + 512]
            pm = em.emit("vector", lambda E, p=pv: E.tensor_mul(p, p, psB[:, 0:512]),
                         deps=[mm, prod_wr[(ic, tt)]], mark=True)
            psU_rd = pm
            prod_wr[(ic, tt)] = pm
        scl_rd = mm

    SDR = 22528
    sd_rd = None
    psC_rd = ps_rd
    x3t_rd = None
    acc_wr = {}
    for et in range(8):
        wls = em.emit("gpsimd", lambda E, e0=et: E.dma_start(
            out=arena[:, SDR:SDR + 5632].rearrange("p (c e) -> p c e", c=22),
            in_=wsd[:, e0 * 256:(e0 + 1) * 256].rearrange("(c p) e -> p c e", p=128)),
            deps=[sd_rd] + ([scl_rd, psU_rd] if et == 0 else []), mark=True, dma=True)
        for tc in range(8):
            mm = None
            for i0 in range(22):
                mm = em.emit("tensor", lambda E, ii=i0, t0=tc: E.matmul(
                    psC[:, 0:256],
                    arena[:, PRS + ii * 1024 + t0 * 128: PRS + ii * 1024 + (t0 + 1) * 128],
                    arena[:, SDR + ii * 256:SDR + (ii + 1) * 256],
                    start=(ii == 0), stop=(ii == 21)),
                    deps=([wls, psC_rd, prod_wr[(21, 1)]] if i0 == 0 else []),
                    mark=(i0 == 21))
            lx = em.emit("scalar", lambda E, t0=tc, e0=et: E.dma_start(
                out=yst32[0][:, 0:256], in_=x3s[t0 * 128:(t0 + 1) * 128,
                                              e0 * 256:(e0 + 1) * 256]),
                deps=[x3t_rd], mark=True, dma=True)
            av = acc[:, tc * D + et * 256: tc * D + (et + 1) * 256]
            ad = em.emit("vector", lambda E, a=av: E.tensor_add(a, yst32[0][:, 0:256],
                                                                psC[:, 0:256]),
                         deps=[mm, lx, acc_wr.get((tc, et))], mark=True)
            psC_rd = ad
            x3t_rd = ad
            acc_wr[(tc, et)] = ad
        sd_rd = mm

    if debug:
        em.emit("sync", lambda E: E.dma_start(out=dbg_wfull, in_=wfull[:]),
                deps=[psR_rd], mark=True, dma=True)
        for t0d in range(8):
            em.emit("sync", lambda E, t1=t0d: E.dma_start(
                out=dbg_acc0[t1 * 128:(t1 + 1) * 128, :], in_=acc[:, t1 * D:(t1 + 1) * D]),
                deps=[ad], mark=True, dma=True)

    HER = 0
    WGC = 11264
    WUC = 13312
    WDR = 15360
    wg_rd, wu_rd, wd_rd = None, None, None
    psA_rd, psB_rd = psG_rd, psU_rd
    he_wr = {}
    psO_rd = psC_rd
    acc_last = ad
    for e in range(8):
        for fc in range(11):
            wlg = em.emit("gpsimd", lambda E, e0=e, f0=fc: E.dma_start(
                out=arena[:, WGC:WGC + 2048].rearrange("p (c e) -> p c e", c=NDC),
                in_=wgt[e0][:, f0 * 128:(f0 + 1) * 128].rearrange("(c p) e -> p c e", p=128)),
                deps=[wg_rd] + ([prod_wr[(10, 1)], sd_rd] if e == 0 and fc == 0 else []),
                mark=True, dma=True)
            wlu = em.emit("gpsimd", lambda E, e0=e, f0=fc: E.dma_start(
                out=arena[:, WUC:WUC + 2048].rearrange("p (c e) -> p c e", c=NDC),
                in_=wut[e0][:, f0 * 128:(f0 + 1) * 128].rearrange("(c p) e -> p c e", p=128)),
                deps=[wu_rd], mark=True, dma=True)
            for tt in range(2):
                mmg = None
                for d0 in range(NDC):
                    mmg = em.emit("tensor", lambda E, dd=d0, t0=tt: E.matmul(
                        psA[:, 0:512], arena[:, WGC + dd * 128:WGC + (dd + 1) * 128],
                        xlnT3[:, dd * TPC + t0 * 512: dd * TPC + t0 * 512 + 512],
                        start=(dd == 0), stop=(dd == NDC - 1)),
                        deps=([wlg, psA_rd] if d0 == 0 else []), mark=(d0 == NDC - 1))
                mmu = None
                for d0 in range(NDC):
                    mmu = em.emit("tensor", lambda E, dd=d0, t0=tt: E.matmul(
                        psB[:, 0:512], arena[:, WUC + dd * 128:WUC + (dd + 1) * 128],
                        xlnT3[:, dd * TPC + t0 * 512: dd * TPC + t0 * 512 + 512],
                        start=(dd == 0), stop=(dd == NDC - 1)),
                        deps=([wlu, psB_rd] if d0 == 0 else []), mark=(d0 == NDC - 1))
                sl = em.emit("scalar", lambda E: E.activation(sil[:], psA[:, 0:512], AF.Silu),
                             deps=[mmg], mark=True)
                psA_rd = sl
                hv = arena[:, HER + fc * 1024 + tt * 512: HER + fc * 1024 + tt * 512 + 512]
                hm = em.emit("vector", lambda E, h0=hv: E.tensor_mul(h0, sil[:], psB[:, 0:512]),
                             deps=[sl, mmu, he_wr.get((fc, tt))], mark=True)
                psB_rd = hm
                he_wr[(fc, tt)] = hm
            wg_rd, wu_rd = mmg, mmu
        for et in range(4):
            wld = em.emit("gpsimd", lambda E, e0=e, f0=et: E.dma_start(
                out=arena[:, WDR:WDR + 5632].rearrange("p (c e) -> p c e", c=11),
                in_=wdt[e0][:, f0 * 512:(f0 + 1) * 512].rearrange("(c p) e -> p c e", p=128)),
                deps=[wd_rd], mark=True, dma=True)
            for tc in range(8):
                mm = None
                for i0 in range(11):
                    mm = em.emit("tensor", lambda E, ii=i0, t0=tc: E.matmul(
                        psC[:, 0:512],
                        arena[:, HER + ii * 1024 + t0 * 128: HER + ii * 1024 + (t0 + 1) * 128],
                        arena[:, WDR + ii * 512:WDR + (ii + 1) * 512],
                        start=(ii == 0), stop=(ii == 10)),
                        deps=([wld, psO_rd, he_wr[(10, 1)]] if i0 == 0 else []),
                        mark=(i0 == 10))
                av = acc[:, tc * D + et * 512: tc * D + (et + 1) * 512]
                ac = em.emit("vector", lambda E, a=av, e0=e, t0=tc: E.scalar_tensor_tensor(
                    a, psC[:, 0:512], wfull[:, t0 * 8 + e0:t0 * 8 + e0 + 1], a,
                    op0=ALU.mult, op1=ALU.add), deps=[mm, acc_last], mark=True)
                psO_rd = ac
                acc_last = ac
            wd_rd = mm

    for tc in range(8):
        em.emit("sync", lambda E, t0=tc: E.dma_start(
            out=outd[t0 * 128:(t0 + 1) * 128, :], in_=acc[:, t0 * D:(t0 + 1) * D]),
            deps=[acc_last], mark=True, dma=True)


def run_l1c(yf_by_core, x2_flat, nw, wo_t, wr_t, wgt, wut, wdt, wsg_t, wsu_t, wsd_t):
    nc6 = _get_prog("l1c", build_l1c)
    in_maps = []
    for c in range(NCORES):
        b = c // 4
        tr = slice((c % 4) * TPC, (c % 4) * TPC + TPC)
        yfc = np.concatenate([yf_by_core[cc][b][:, tr] for cc in range(8)], 0)
        in_maps.append({
            "yfc": np.ascontiguousarray(yfc),
            "xs": x2_flat[c * TPC:(c + 1) * TPC],
            "nw": nw, "wo": wo_t, "wr": wr_t, "wgt": wgt, "wut": wut,
            "wdt": wdt, "wsg": wsg_t, "wsu": wsu_t, "wsd": wsd_t,
        })
    return run_bass_kernel_spmd(nc6, in_maps, core_ids=list(range(NCORES)))
